# version 01
# baseline (speedup 1.0000x reference)
import numpy as np
from collections import defaultdict
from contextlib import ExitStack

import concourse.bass as bass
import concourse.mybir as mybir
from concourse.alu_op_type import AluOpType as ALU
from concourse.bass_utils import run_bass_kernel_spmd

F32 = mybir.dt.float32
F32R = mybir.dt.float32r
BF16 = mybir.dt.bfloat16
U32 = mybir.dt.uint32
AF = mybir.ActivationFunctionType

D = 1024
DC = 8
TPC = 2048
NB = 4
NCORES = 8
GROUPS = [[0, 1], [2, 3], [4, 5], [6, 7]]

GA = 1.5957691216057308
GB = 0.07135481282952736
DBG_FFN0 = False
NO_COLL = False
ZONES = list(range(8))
FBS = None



ENGINES = ("pe", "act", "dve", "pool", "sp")
DEDUP_WAITS = True


class _Op:
    __slots__ = ("engine", "fn", "tag", "is_dma", "stream", "deps",
                 "needs_inc", "inc_value", "idx")

    def __init__(self, engine, fn, tag=""):
        self.engine = engine
        self.fn = fn
        self.tag = tag
        self.is_dma = False
        self.stream = None
        self.deps = []
        self.needs_inc = False
        self.inc_value = None
        self.idx = -1


class Prog:
    def __init__(self, nc, ctx):
        self.nc = nc
        self.ctx = ctx
        self.all_ops = []
        self.eng_sem = {e: ctx.enter_context(nc.semaphore(f"S_{e}")) for e in ENGINES}
        self.dma_sems = {}
        self.last_write = {}
        self.readers = defaultdict(list)
        self._alias_guard = []

    def stream_sem(self, name):
        if name not in self.dma_sems:
            h = self.ctx.enter_context(self.nc.semaphore(f"D_{name}"))
            self.dma_sems[name] = [h, 0]
        return self.dma_sems[name]

    @staticmethod
    def key_of(x):
        if isinstance(x, str):
            return x
        if hasattr(x, "tensor"):
            return x.tensor.name
        if hasattr(x, "name"):
            return x.name
        raise TypeError(f"cannot key {x!r}")

    def _add_dep(self, o, producer):
        if producer is o:
            return
        if producer.is_dma:
            val = self.dma_sems[producer.stream][1]
            o.deps.append((producer, val))
        else:
            producer.needs_inc = True
            o.deps.append((producer, None))

    def _record(self, o, reads, writes, psum_accum=False):
        rkeys = [self.key_of(r) for r in reads]
        wkeys = [self.key_of(w) for w in writes]
        for k in rkeys:
            lw = self.last_write.get(k)
            if lw is not None:
                self._add_dep(o, lw)
        for k in wkeys:
            lw = self.last_write.get(k)
            if lw is not None and not (psum_accum and o.engine == "pe"):
                self._add_dep(o, lw)
            elif lw is None:
                for g in self._alias_guard:
                    self._add_dep(o, g)
            for r in self.readers[k]:
                self._add_dep(o, r)
        for k in rkeys:
            self.readers[k].append(o)
        for k in wkeys:
            self.last_write[k] = o
            self.readers[k] = []
        o.idx = len(self.all_ops)
        self.all_ops.append(o)
        return o

    def op(self, engine, fn, reads=(), writes=(), tag="", psum_accum=False):
        return self._record(_Op(engine, fn, tag), reads, writes, psum_accum)

    def dma(self, engine, out, in_, stream, reads=(), writes=(), tag="", **kw):
        reads = list(reads) if reads else [in_]
        writes = list(writes) if writes else [out]
        semrec = self.stream_sem(stream)
        o = _Op(engine, None, tag)
        o.is_dma = True
        o.stream = stream
        o.fn = lambda eng, _o=out, _i=in_, _kw=kw: eng.dma_start(out=_o, in_=_i, **_kw)
        self._record(o, reads, writes)
        semrec[1] += 16
        return o

    def indirect_dma(self, out, out_offset, in_, in_offset, stream,
                     reads=(), writes=(), tag="", **kw):
        semrec = self.stream_sem(stream)
        o = _Op("pool", None, tag)
        o.is_dma = True
        o.stream = stream
        o.fn = lambda eng: eng.indirect_dma_start(out, out_offset, in_, in_offset, **kw)
        self._record(o, list(reads), list(writes))
        semrec[1] += 16
        return o

    def collective(self, kind, op, ins, outs, reads, writes, stream=None, tag=""):
        o = _Op("pool", None, tag)
        o.fn = lambda eng: eng.collective_compute(kind, op, replica_groups=GROUPS,
                                                  ins=ins, outs=outs)
        return self._record(o, list(reads) + ["__coll__"],
                            list(writes) + ["__coll__"])

    def barrier(self, tag="barrier"):
        frontier = set()
        for o in self.last_write.values():
            frontier.add(id(o))
        lasts = {id(o): o for o in self.last_write.values()}
        for rl in self.readers.values():
            for o in rl:
                lasts[id(o)] = o
        lasts = list(lasts.values())
        outs = []
        for e in ENGINES:
            o = _Op(e, lambda eng: None, f"{tag}_{e}")
            for p in lasts:
                self._add_dep(o, p)
            o.idx = len(self.all_ops)
            self.all_ops.append(o)
            outs.append(o)
        self._alias_guard = outs
        return outs

    def wait_all_stores(self, engine="pool"):
        o = _Op(engine, lambda eng: None, "final_wait")
        o.deps = [("__stream__", name) for name in self.dma_sems]
        o.idx = len(self.all_ops)
        self.all_ops.append(o)
        return o

    def finalize(self, block):
        counters = {e: 0 for e in ENGINES}
        for o in self.all_ops:
            if o.is_dma:
                continue
            if o.needs_inc:
                counters[o.engine] += 1
                o.inc_value = counters[o.engine]

        per_engine = {e: [] for e in ENGINES}
        for o in self.all_ops:
            per_engine[o.engine].append(o)

        waited = {e: defaultdict(int) for e in ENGINES}
        resolved = {}
        for e in ENGINES:
            for o in per_engine[e]:
                waits = []
                for dep in o.deps:
                    if isinstance(dep[0], str) and dep[0] == "__stream__":
                        h, tot = self.dma_sems[dep[1]]
                        if tot > 0:
                            waits.append((h, tot))
                        continue
                    p, frozen = dep
                    if p.is_dma:
                        sem = self.dma_sems[p.stream][0]
                        val = frozen
                    else:
                        sem = self.eng_sem[p.engine]
                        val = p.inc_value
                        assert val is not None, f"dep on op {p.tag} lacks inc"
                    name = sem.name if hasattr(sem, "name") else id(sem)
                    if DEDUP_WAITS and waited[e][name] >= val:
                        continue
                    waited[e][name] = val
                    waits.append((sem, val))
                resolved[id(o)] = waits

        engmap = dict(pe="tensor", act="scalar", dve="vector", pool="gpsimd",
                      sp="sync")

        def _mk(engine):
            def _(eng_obj):
                for o in per_engine[engine]:
                    for (sem, val) in resolved[id(o)]:
                        eng_obj.wait_ge(sem, val)
                    inst = o.fn(eng_obj)
                    if o.is_dma:
                        inst.then_inc(self.dma_sems[o.stream][0], 16)
                    elif o.needs_inc:
                        if inst is None:
                            inst = eng_obj.nop()
                        inst.then_inc(self.eng_sem[o.engine], 1)
            return _

        for engine in ENGINES:
            if not per_engine[engine]:
                continue
            getattr(block, engmap[engine])(_mk(engine))



class Env:
    pass


def alloc_common(nc, ctx, E):
    E.ones_r = nc.dram_tensor("c_ones_r", [1, 128], F32, kind="ExternalInput").ap()
    E.ones_c = nc.dram_tensor("c_ones_c", [128, 1], F32, kind="ExternalInput").ap()
    E.ident = nc.dram_tensor("c_ident", [128, 128], F32, kind="ExternalInput").ap()
    ent = ctx.enter_context
    E.ones_r_sb = ent(nc.sbuf_tensor("ones_r_sb", [1, 128], F32))
    E.ones_c_sb = ent(nc.sbuf_tensor("ones_c_sb", [128, 1], F32))
    E.ident_sb = ent(nc.sbuf_tensor("ident_sb", [128, 128], F32))


def load_common(E):
    P = E.P
    P.dma("sp", E.ones_r_sb[:], E.ones_r, "ldc", writes=[E.ones_r_sb])
    P.dma("sp", E.ones_c_sb[:], E.ones_c, "ldc", writes=[E.ones_c_sb])
    P.dma("sp", E.ident_sb[:], E.ident, "ldc", writes=[E.ident_sb])


def common_host_inputs():
    return {
        "c_ones_r": np.ones((1, 128), np.float32),
        "c_ones_c": np.ones((128, 1), np.float32),
        "c_ident": np.eye(128, dtype=np.float32),
    }


def emit_ln(E, xin_dram, xh_sb, tag=""):
    emit_ln_stats(E, xin_dram, tag=tag)
    emit_ln_norm(E, xin_dram, xh_sb, tag=tag)


def emit_ln_stats(E, xin_dram, tag=""):
    nc, P = E.nc, E.P
    xc, sqs = E.xc, E.sqs
    ps_s = E.ps_stat1
    ps_q = E.ps_stat2
    ps_bc = E.ps_bc
    srow, qrow, mrow, rrow = E.srow, E.qrow, E.mrow, E.rrow
    mb, rb = E.mb, E.rb
    t1, t2 = E.t1, E.t2

    xin_r = xin_dram.rearrange("(c p) t -> p c t", p=128)
    for tb in range(4):
        sl = slice(tb * 512, (tb + 1) * 512)
        for c in range(DC):
            slot = c % 2
            P.dma("sp", xc[slot][:, 0:512], xin_r[:, c, sl], f"ldx{tag}_{slot}",
                  writes=[xc[slot]], tag=f"{tag}ln_ldx{tb}_{c}")
            P.op("act", lambda s, slot=slot: s.activation(
                sqs[slot][:, 0:512], xc[slot][:, 0:512], AF.Square),
                 reads=[xc[slot]], writes=[sqs[slot]], tag=f"{tag}ln_sq{tb}_{c}")
            P.op("pe", lambda t, c=c, slot=slot: t.matmul(
                ps_s[:], E.ones_c_sb[:], xc[slot][:, 0:512],
                start=(c == 0), stop=(c == DC - 1)),
                 reads=[E.ones_c_sb, xc[slot]], writes=[ps_s],
                 psum_accum=(c > 0), tag=f"{tag}ln_s{tb}_{c}")
            P.op("pe", lambda t, c=c, slot=slot: t.matmul(
                ps_q[:], E.ones_c_sb[:], sqs[slot][:, 0:512],
                start=(c == 0), stop=(c == DC - 1)),
                 reads=[E.ones_c_sb, sqs[slot]], writes=[ps_q],
                 psum_accum=(c > 0), tag=f"{tag}ln_q{tb}_{c}")
        P.op("act", lambda s, sl=sl: s.copy(srow[:, sl], ps_s[:]),
             reads=[ps_s], writes=[srow], tag=f"{tag}ln_es{tb}")
        P.op("act", lambda s, sl=sl: s.copy(qrow[:, sl], ps_q[:]),
             reads=[ps_q], writes=[qrow], tag=f"{tag}ln_eq{tb}")
    P.op("dve", lambda v: v.tensor_scalar(mrow[:], srow[:], 1.0 / D, None, ALU.mult),
         reads=[srow], writes=[mrow], tag=f"{tag}ln_m")
    P.op("dve", lambda v: v.tensor_scalar(qrow[:], qrow[:], 1.0 / D, None, ALU.mult),
         reads=[qrow], writes=[qrow], tag=f"{tag}ln_q1")
    P.op("dve", lambda v: v.tensor_tensor(t1[0][:1, :TPC], mrow[:], mrow[:], ALU.mult),
         reads=[mrow], writes=[t1[0]], tag=f"{tag}ln_m2")
    P.op("dve", lambda v: v.tensor_tensor(qrow[:], qrow[:], t1[0][:1, :TPC], ALU.subtract),
         reads=[qrow, t1[0]], writes=[qrow], tag=f"{tag}ln_var")
    P.op("dve", lambda v: v.tensor_scalar(qrow[:], qrow[:], 1e-5, None, ALU.add),
         reads=[qrow], writes=[qrow], tag=f"{tag}ln_eps")
    P.op("act", lambda s: s.activation(t1[0][:1, :TPC], qrow[:], AF.Ln),
         reads=[qrow], writes=[t1[0]], tag=f"{tag}ln_ln")
    P.op("dve", lambda v: v.tensor_scalar(t1[0][:1, :TPC], t1[0][:1, :TPC], -0.5,
                                          None, ALU.mult),
         reads=[t1[0]], writes=[t1[0]], tag=f"{tag}ln_half")
    P.op("act", lambda s: s.activation(rrow[:], t1[0][:1, :TPC], AF.Exp),
         reads=[t1[0]], writes=[rrow], tag=f"{tag}ln_exp")
    P.op("dve", lambda v: v.tensor_tensor(t1[0][:1, :TPC], rrow[:], rrow[:], ALU.mult),
         reads=[rrow], writes=[t1[0]], tag=f"{tag}ln_y2")
    P.op("dve", lambda v: v.tensor_scalar(t2[0][:1, :TPC], qrow[:], -0.5, None,
                                          ALU.mult),
         reads=[qrow], writes=[t2[0]], tag=f"{tag}ln_hv")
    P.op("dve", lambda v: v.tensor_tensor(t1[0][:1, :TPC], t1[0][:1, :TPC],
                                          t2[0][:1, :TPC], ALU.mult),
         reads=[t1[0], t2[0]], writes=[t1[0]], tag=f"{tag}ln_vy2")
    P.op("dve", lambda v: v.tensor_scalar(t1[0][:1, :TPC], t1[0][:1, :TPC], 1.5, None,
                                          ALU.add),
         reads=[t1[0]], writes=[t1[0]], tag=f"{tag}ln_15")
    P.op("dve", lambda v: v.tensor_tensor(rrow[:], rrow[:], t1[0][:1, :TPC], ALU.mult),
         reads=[rrow, t1[0]], writes=[rrow], tag=f"{tag}ln_newton")
    for (row, dst, nm) in ((mrow, mb, "m"), (rrow, rb, "r")):
        for tb in range(4):
            sl = slice(tb * 512, (tb + 1) * 512)
            P.op("pe", lambda t, row=row, sl=sl: t.matmul(
                ps_bc[:], E.ones_r_sb[:], row[:1, sl], start=True, stop=True),
                 reads=[E.ones_r_sb, row], writes=[ps_bc], tag=f"{tag}ln_bc{nm}{tb}")
            P.op("act", lambda s, dst=dst, sl=sl: s.copy(dst[:, sl], ps_bc[:]),
                 reads=[ps_bc], writes=[dst], tag=f"{tag}ln_bce{nm}{tb}")


def emit_ln_norm(E, xin_dram, xh_sb, tag="", chunks=None, dst_stride=TPC,
                 tb_range=range(4)):
    P = E.P
    xc, mb, rb = E.xc, E.mb, E.rb
    xin_r = xin_dram.rearrange("(c p) t -> p c t", p=128)
    if chunks is None:
        chunks = range(DC)
    for ci, c in enumerate(chunks):
        for tb in tb_range:
            sl = slice(tb * 512, (tb + 1) * 512)
            slot = (ci * 4 + tb) % 2
            osl = slice(ci * dst_stride + tb * 512, ci * dst_stride + (tb + 1) * 512)
            P.dma("sp", xc[slot][:, 0:512], xin_r[:, c, sl], f"ldx{tag}_{slot}",
                  writes=[xc[slot]], tag=f"{tag}ln_ldx2_{c}_{tb}")
            P.op("dve", lambda v, osl=osl, slot=slot, sl=sl: v.tensor_tensor(
                xh_sb[:, osl], xc[slot][:, 0:512], mb[:, sl], ALU.subtract),
                 reads=[xc[slot], mb], writes=[xh_sb], tag=f"{tag}ln_sub{c}_{tb}")
            P.op("dve", lambda v, osl=osl, sl=sl: v.tensor_tensor(
                xh_sb[:, osl], xh_sb[:, osl], rb[:, sl], ALU.mult),
                 reads=[xh_sb, rb], writes=[xh_sb], tag=f"{tag}ln_mul{c}_{tb}")


def alloc_ln(nc, ctx, E):
    ent = ctx.enter_context
    E.xc = [ent(nc.sbuf_tensor(f"xc{i}", [128, 512], F32)) for i in range(2)]
    E.sqs = [ent(nc.sbuf_tensor(f"sqs{i}", [128, 512], F32)) for i in range(2)]
    E.ps_stat1 = ent(nc.psum_tensor("ps_s", [1, 512], F32))
    E.ps_stat2 = ent(nc.psum_tensor("ps_q", [1, 512], F32))
    E.ps_bc = ent(nc.psum_tensor("ps_bc", [128, 512], F32))
    E.mrow = ent(nc.sbuf_tensor("mrow", [1, TPC], F32))
    E.rrow = ent(nc.sbuf_tensor("rrow", [1, TPC], F32))
    E.mb = ent(nc.sbuf_tensor("mb", [128, TPC], F32))
    E.rb = ent(nc.sbuf_tensor("rb", [128, TPC], F32))
    E.t1 = [ent(nc.sbuf_tensor("lnt1", [1, TPC], F32))]
    E.t2 = [ent(nc.sbuf_tensor("lnt2", [1, TPC], F32))]
    E.srow = E.t1[0]
    E.qrow = E.t2[0]


def emit_gelu_chain(E, z_ps, out_sb_slice, scratch, tag):
    P = E.P
    P.op("act", lambda s: s.activation(scratch[:], z_ps[:], AF.Square),
         reads=[z_ps], writes=[scratch], tag=f"{tag}_sq")
    P.op("dve", lambda v: v.tensor_scalar(scratch[:], scratch[:], GB, GA,
                                          ALU.mult, ALU.add),
         reads=[scratch], writes=[scratch], tag=f"{tag}_p2")
    P.op("dve", lambda v: v.tensor_tensor(scratch[:], scratch[:], z_ps[:], ALU.mult),
         reads=[scratch, z_ps], writes=[scratch], tag=f"{tag}_p3")
    P.op("act", lambda s: s.activation(scratch[:], scratch[:], AF.Sigmoid),
         reads=[scratch], writes=[scratch], tag=f"{tag}_sig")
    P.op("dve", lambda v: v.tensor_tensor(out_sb_slice, scratch[:], z_ps[:], ALU.mult),
         reads=[scratch, z_ps], writes=[out_sb_slice.tensor.name], tag=f"{tag}_h")



FFB = 512
NFFB = 4096 // FFB
TBZ = 512
NTBZ = TPC // TBZ


def build_ffn0():
    nc = bass.Bass("TRN2", target_bir_lowering=False)
    E = Env()
    E.nc = nc
    xin = nc.dram_tensor("xin", [D, TPC], F32, kind="ExternalInput").ap()
    w1 = nc.dram_tensor("w1", [D, 256], F32, kind="ExternalInput").ap()
    b1 = nc.dram_tensor("b1", [1, 256], F32, kind="ExternalInput").ap()
    w2 = nc.dram_tensor("w2", [256, 8], F32, kind="ExternalInput").ap()
    b2 = nc.dram_tensor("b2", [1, 8], F32, kind="ExternalInput").ap()
    wa = nc.dram_tensor("wa", [8, D, 4096], F32, kind="ExternalInput").ap()
    wb = nc.dram_tensor("wb", [8, 4096, D], F32, kind="ExternalInput").ap()
    xout = nc.dram_tensor("xout", [D, TPC], F32, kind="ExternalOutput").ap()
    pool_dram = nc.dram_tensor("pool_d", [128, 8], F32, kind="Internal").ap()
    pool_out = nc.dram_tensor("pool_o", [128, 8], F32, kind="Internal").ap()

    with ExitStack() as ctx:
        ent = ctx.enter_context
        P = Prog(nc, ctx)
        E.P = P
        alloc_common(nc, ctx, E)
        alloc_ln(nc, ctx, E)
        xh = ent(nc.sbuf_tensor("xh", [128, DC * TPC], F32))
        w1_sb = ent(nc.sbuf_tensor("w1_sb", [128, 8 * 256], F32))
        b1_sb = ent(nc.sbuf_tensor("b1_sb", [1, 256], F32))
        w2_sb = ent(nc.sbuf_tensor("w2_sb", [128, 2 * 8], F32))
        b2_sb = ent(nc.sbuf_tensor("b2_sb", [1, 8], F32))
        pool_sb = ent(nc.sbuf_tensor("pool_sb", [128, 8], F32))
        hrow = ent(nc.sbuf_tensor("hrow", [1, 256], F32))
        htt = ent(nc.sbuf_tensor("htt", [128, 2], F32))
        lgrow = ent(nc.sbuf_tensor("lgrow", [1, 16], F32))
        wz = ent(nc.sbuf_tensor("wz", [128, 8], F32))
        wa_sb = [ent(nc.sbuf_tensor("wa_sb0", [128, DC * FFB], F32))]
        wb_sb = [ent(nc.sbuf_tensor("wb_sb0", [128, (FFB // 128) * D], F32))]
        hblk = ent(nc.sbuf_tensor("hblk", [128, (FFB // 128) * TBZ], F32))
        yblk = ent(nc.sbuf_tensor("yblk", [128, DC * TBZ], F32))
        gsc = [ent(nc.sbuf_tensor(f"gsc{i}", [128, TBZ], F32)) for i in range(2)]
        ps_z = [ent(nc.psum_tensor(f"ps_z{i}", [128, TBZ], F32)) for i in range(2)]
        ps_y = [ent(nc.psum_tensor(f"ps_y{i}", [128, TBZ], F32)) for i in range(2)]
        ps_r = E.ps_stat1
        ps_tr = E.ps_bc
        outc = E.xc
        if DBG_FFN0:
            E.dbg_sb = E.mb
        block = ent(nc.Block())

        load_common(E)
        P.dma("sp", w1_sb[:].rearrange("p (c n) -> p c n", c=8),
              w1.rearrange("(c p) n -> p c n", p=128), "ldw", writes=[w1_sb])
        P.dma("sp", b1_sb[:], b1, "ldw", writes=[b1_sb])
        P.dma("sp", w2_sb[:, 0:16].rearrange("p (c n) -> p c n", c=2),
              w2.rearrange("(c p) n -> p c n", p=128), "ldw", writes=[w2_sb])
        P.dma("sp", b2_sb[:], b2, "ldw", writes=[b2_sb])

        emit_ln(E, xin, xh, tag="f0")

        for c in range(DC):
            P.op("dve", lambda v, c=c: v.reduce_sum(
                pool_sb[:, c:c + 1], xh[:, c * TPC:(c + 1) * TPC],
                axis=mybir.AxisListType.X),
                 reads=[xh], writes=[pool_sb], tag=f"pool{c}")
        P.dma("pool", pool_dram, pool_sb[:], "poolw", reads=[pool_sb],
              writes=["pool_d"], tag="pool_w")
        if NO_COLL:
            P.dma("pool", pool_out, pool_dram, "collp", reads=["pool_d"],
                  writes=["pool_o"], tag="pool_cp")
        else:
            P.collective("AllReduce", ALU.add, [pool_dram], [pool_out],
                         reads=["pool_d"], writes=["pool_o"], stream="collp",
                         tag="pool_ar")
        P.dma("sp", pool_sb[:], pool_out, "poolr", reads=["pool_o"],
              writes=[pool_sb], tag="pool_r")
        P.op("dve", lambda v: v.tensor_scalar(pool_sb[:], pool_sb[:], 1.0 / 4096, None,
                                              ALU.mult),
             reads=[pool_sb], writes=[pool_sb], tag="pool_scale")

        for c in range(DC):
            P.op("pe", lambda t, c=c: t.matmul(
                ps_r[:1, 0:256], pool_sb[:, c:c + 1], w1_sb[:, c * 256:(c + 1) * 256],
                start=(c == 0), stop=(c == DC - 1)),
                 reads=[pool_sb, w1_sb], writes=[ps_r], psum_accum=(c > 0),
                 tag=f"rt_h{c}")
        P.op("act", lambda s: s.copy(hrow[:], ps_r[:1, 0:256]), reads=[ps_r],
             writes=[hrow], tag="rt_he")
        P.op("dve", lambda v: v.tensor_tensor(hrow[:], hrow[:], b1_sb[:], ALU.add),
             reads=[hrow, b1_sb], writes=[hrow], tag="rt_b1")
        P.op("act", lambda s: s.activation(hrow[:], hrow[:], AF.Tanh),
             reads=[hrow], writes=[hrow], tag="rt_tanh")
        for j in range(2):
            P.op("pe", lambda t, j=j: t.transpose(
                ps_tr[:, j:j + 1], hrow[:1, j * 128:(j + 1) * 128],
                E.ident_sb[0:1, 0:1]),
                 reads=[hrow, E.ident_sb], writes=[ps_tr], tag=f"rt_tr{j}")
            P.op("act", lambda s, j=j: s.copy(htt[:, j:j + 1], ps_tr[:, j:j + 1]),
                 reads=[ps_tr], writes=[htt], tag=f"rt_tre{j}")
        for j in range(2):
            P.op("pe", lambda t, j=j: t.matmul(
                ps_r[:1, 384:392], htt[:, j:j + 1], w2_sb[:, j * 8:(j + 1) * 8],
                start=(j == 0), stop=(j == 1)),
                 reads=[htt, w2_sb], writes=[ps_r], psum_accum=(j > 0),
                 tag=f"rt_lg{j}")
        P.op("act", lambda s: s.copy(lgrow[:1, 0:8], ps_r[:1, 384:392]),
             reads=[ps_r], writes=[lgrow], tag="rt_lge")
        P.op("dve", lambda v: v.tensor_tensor(lgrow[:1, 0:8], lgrow[:1, 0:8],
                                              b2_sb[:], ALU.add),
             reads=[lgrow, b2_sb], writes=[lgrow], tag="rt_b2")
        P.op("dve", lambda v: v.tensor_reduce(lgrow[:1, 8:9], lgrow[:1, 0:8],
                                              mybir.AxisListType.X, ALU.max),
             reads=[lgrow], writes=[lgrow], tag="rt_max")
        P.op("dve", lambda v: v.tensor_scalar(lgrow[:1, 8:9], lgrow[:1, 8:9], -1.0,
                                              None, ALU.mult),
             reads=[lgrow], writes=[lgrow], tag="rt_negmax")
        P.op("act", lambda s: s.activation(lgrow[:1, 0:8], lgrow[:1, 0:8], AF.Exp,
                                           bias=lgrow[:1, 8:9]),
             reads=[lgrow], writes=[lgrow], tag="rt_exp")
        P.op("dve", lambda v: v.reduce_sum(lgrow[:1, 8:9], lgrow[:1, 0:8],
                                           axis=mybir.AxisListType.X),
             reads=[lgrow], writes=[lgrow], tag="rt_sum")
        P.op("dve", lambda v: v.reciprocal(lgrow[:1, 9:10], lgrow[:1, 8:9]),
             reads=[lgrow], writes=[lgrow], tag="rt_rcp")
        P.op("dve", lambda v: v.tensor_scalar(lgrow[:1, 0:8], lgrow[:1, 0:8],
                                              lgrow[:1, 9:10], None, ALU.mult),
             reads=[lgrow], writes=[lgrow], tag="rt_probs")
        P.op("dve", lambda v: v.max(out=lgrow[:1, 8:16], in_=lgrow[:1, 0:8]),
             reads=[lgrow], writes=[lgrow], tag="rt_max8")
        P.op("dve", lambda v: v.tensor_scalar(E.t2[0][:1, 0:8], lgrow[:1, 0:8],
                                              lgrow[:1, 10:11], None, ALU.is_ge),
             reads=[lgrow], writes=[E.t2[0]], tag="rt_mask")
        P.op("dve", lambda v: v.tensor_tensor(lgrow[:1, 0:8], lgrow[:1, 0:8],
                                              E.t2[0][:1, 0:8], ALU.mult,),
             reads=[lgrow, E.t2[0]], writes=[lgrow], tag="rt_w")
        P.op("dve", lambda v: v.reduce_sum(lgrow[:1, 8:9], lgrow[:1, 0:8],
                                           axis=mybir.AxisListType.X),
             reads=[lgrow], writes=[lgrow], tag="rt_wsum")
        P.op("dve", lambda v: v.reciprocal(lgrow[:1, 9:10], lgrow[:1, 8:9]),
             reads=[lgrow], writes=[lgrow], tag="rt_wrcp")
        P.op("dve", lambda v: v.tensor_scalar(lgrow[:1, 0:8], lgrow[:1, 0:8],
                                              lgrow[:1, 9:10], None, ALU.mult),
             reads=[lgrow], writes=[lgrow], tag="rt_wnorm")
        P.op("pe", lambda t: t.matmul(E.ps_bc[:, 0:8], E.ones_r_sb[:], lgrow[:1, 0:8],
                                      start=True, stop=True),
             reads=[E.ones_r_sb, lgrow], writes=[E.ps_bc], tag="rt_bcw")
        P.op("act", lambda s: s.copy(wz[:], E.ps_bc[:, 0:8]), reads=[E.ps_bc],
             writes=[wz], tag="rt_bcwe")

        NFC = FFB // 128
        for tbz in range(NTBZ):
            tsl = slice(tbz * TBZ, (tbz + 1) * TBZ)
            for dch in range(DC):
                P.op("dve", lambda v, dch=dch, tbz=tbz: v.tensor_copy(
                    yblk[:, dch * TBZ:(dch + 1) * TBZ],
                    xh[:, dch * TPC + tbz * TBZ:dch * TPC + tbz * TBZ + TBZ]),
                     reads=[xh], writes=[yblk], tag=f"y0_{tbz}_{dch}")
            for e in ZONES:
                for fb in (FBS if FBS is not None else range(NFFB)):
                    wslot = 0
                    P.dma("sp", wa_sb[wslot][:].rearrange("p (c f) -> p c f", c=DC),
                          wa[e, :, fb * FFB:(fb + 1) * FFB]
                          .rearrange("(c p) f -> p c f", p=128),
                          "ldwa", writes=[wa_sb[wslot]], tag=f"lwa{tbz}_{e}_{fb}")
                    P.dma("sp", wb_sb[wslot][:].rearrange("p (c f) -> p c f", c=NFC),
                          wb[e, fb * FFB:(fb + 1) * FFB, :]
                          .rearrange("(c p) f -> p c f", p=128),
                          "ldwb", writes=[wb_sb[wslot]], tag=f"lwb{tbz}_{e}_{fb}")
                    for fc in range(NFC):
                        zslot = fc % 2
                        for dch in range(DC):
                            P.op("pe", lambda t, fc=fc, dch=dch, w=wa_sb[wslot],
                                 zs=ps_z[zslot], a=dch * TPC + tsl.start,
                                 b=dch * TPC + tsl.stop: t.matmul(
                                     zs[:], w[:, dch * FFB + fc * 128:
                                              dch * FFB + fc * 128 + 128],
                                     xh[:, a:b],
                                     start=(dch == 0), stop=(dch == DC - 1)),
                                 reads=[wa_sb[wslot], xh], writes=[ps_z[zslot]],
                                 psum_accum=(dch > 0), tag=f"za{tbz}_{e}_{fb}_{fc}_{dch}")
                        emit_gelu_chain(E, ps_z[zslot],
                                        hblk[:, fc * TBZ:(fc + 1) * TBZ],
                                        gsc[zslot], f"g{tbz}_{e}_{fb}_{fc}")
                    for dch in range(DC):
                        yslot = dch % 2
                        for fc in range(NFC):
                            P.op("pe", lambda t, fc=fc, dch=dch, w=wb_sb[wslot],
                                 ys=ps_y[yslot]: t.matmul(
                                     ys[:], w[:, fc * D + dch * 128:
                                              fc * D + dch * 128 + 128],
                                     hblk[:, fc * TBZ:(fc + 1) * TBZ],
                                     start=(fc == 0), stop=(fc == NFC - 1)),
                                 reads=[wb_sb[wslot], hblk], writes=[ps_y[yslot]],
                                 psum_accum=(fc > 0), tag=f"zb{tbz}_{e}_{fb}_{dch}_{fc}")
                        P.op("dve", lambda v, dch=dch, e=e, ys=ps_y[yslot]:
                             v.scalar_tensor_tensor(
                                 yblk[:, dch * TBZ:(dch + 1) * TBZ], ys[:],
                                 wz[:, e:e + 1], yblk[:, dch * TBZ:(dch + 1) * TBZ],
                                 ALU.mult, ALU.add),
                             reads=[ps_y[yslot], wz, yblk], writes=[yblk],
                             tag=f"yacc{tbz}_{e}_{fb}_{dch}")
            if DBG_FFN0:
                ydbg_d = nc.dram_tensor(f"ydbg{tbz}", [128, DC * TBZ], F32,
                                        kind="ExternalOutput").ap()
                for hh in range(2):
                    P.op("dve", lambda v, hh=hh: v.tensor_copy(
                        E.dbg_sb[:], yblk[:, hh * 2048:(hh + 1) * 2048]),
                         reads=[yblk], writes=[E.dbg_sb], tag=f"ydbgc{tbz}_{hh}")
                    P.dma("pool", ydbg_d[:, hh * 2048:(hh + 1) * 2048], E.dbg_sb[:],
                          f"ydbgs{tbz}_{hh}", reads=[E.dbg_sb],
                          writes=[f"ydbg{tbz}"], tag=f"ydbgs{tbz}_{hh}")
            for dch in range(DC):
                oslot = dch % 2
                P.dma("sp", outc[oslot][:, 0:512],
                      xin.rearrange("(c p) t -> p c t", p=128)[:, dch, tsl],
                      f"ldxf_{oslot}", writes=[outc[oslot]], tag=f"fin_ld{tbz}_{dch}")
                P.op("dve", lambda v, dch=dch, o=outc[oslot]: v.scalar_tensor_tensor(
                    o[:, 0:512], yblk[:, dch * TBZ:(dch + 1) * TBZ], 0.5, o[:, 0:512],
                    ALU.mult, ALU.add),
                     reads=[yblk, outc[oslot]], writes=[outc[oslot]],
                     tag=f"fin_add{tbz}_{dch}")
                P.dma("pool", xout.rearrange("(c p) t -> p c t", p=128)[:, dch, tsl],
                      outc[oslot][:, 0:512], f"stx_{oslot}", reads=[outc[oslot]],
                      writes=["xout"], tag=f"fin_st{tbz}_{dch}")

        P.wait_all_stores("pool")
        P.finalize(block)
    return nc




def build_attn1():
    nc = bass.Bass("TRN2", target_bir_lowering=False)
    E = Env()
    E.nc = nc
    xin = nc.dram_tensor("xin", [D, TPC], F32, kind="ExternalInput").ap()
    tdsb = nc.dram_tensor("tdsb", [128, 8], F32, kind="ExternalInput").ap()
    tmv_d = nc.dram_tensor("tmv", [128, 8], F32, kind="ExternalInput").ap()
    tmr_d = nc.dram_tensor("tmr", [128, 8], F32, kind="ExternalInput").ap()
    wvt_d = nc.dram_tensor("wvt", [D, D], F32, kind="ExternalInput").ap()
    wrt_d = nc.dram_tensor("wrt", [D, D], F32, kind="ExternalInput").ap()
    wot_d = nc.dram_tensor("wot", [D, D], F32, kind="ExternalInput").ap()
    ffl = nc.dram_tensor("ffl", [128, 1], F32, kind="ExternalInput").ap()
    r0_d = nc.dram_tensor("c_r0", [128, TPC], F32, kind="ExternalInput").ap()
    xout = nc.dram_tensor("xout", [D, TPC], F32, kind="ExternalOutput").ap()
    cdr = nc.dram_tensor("cdr", [128, 8], F32, kind="Internal").ap()
    agc = nc.dram_tensor("agc", [256, 8], F32, kind="Internal").ap()

    with ExitStack() as ctx:
        ent = ctx.enter_context
        P = Prog(nc, ctx)
        E.P = P
        alloc_common(nc, ctx, E)
        alloc_ln(nc, ctx, E)
        prev = ent(nc.sbuf_tensor("prev", [128, DC * TPC], F32))
        xhc = ent(nc.sbuf_tensor("xhc", [128, TPC], F32))
        t3 = xhc
        vin = ent(nc.sbuf_tensor("vin", [128, DC * 512], F32))
        rin = ent(nc.sbuf_tensor("rin", [128, DC * 512], F32))
        v_sb = ent(nc.sbuf_tensor("v_sb", [128, DC * 512], F32))
        r_sb = ent(nc.sbuf_tensor("r_sb", [128, DC * 512], F32))
        r0 = v_sb
        wsl = ent(nc.sbuf_tensor("wsl", [128, DC * 256], F32))
        dcol = ent(nc.sbuf_tensor("dcol", [128, 8], F32))
        lnd = ent(nc.sbuf_tensor("lnd", [128, 8], F32))
        slast = ent(nc.sbuf_tensor("slast", [128, 8], F32))
        carry = ent(nc.sbuf_tensor("carry", [128, 8], F32))
        fcol = ent(nc.sbuf_tensor("fcol", [128, 1], F32))
        tmvs = ent(nc.sbuf_tensor("tmvs", [128, 8], F32))
        tmrs = ent(nc.sbuf_tensor("tmrs", [128, 8], F32))
        dd = ent(nc.sbuf_tensor("ddm", [128, 512], F32))
        tcmb = [ent(nc.sbuf_tensor(f"tcmb{i}", [128, 512], F32)) for i in range(2)]
        ps_p = [ent(nc.psum_tensor(f"ps_p{i}", [128, 512], F32)) for i in range(2)]
        outc = E.xc
        block = ent(nc.Block())

        load_common(E)
        P.dma("sp", r0[:, 0:TPC], r0_d, "ldc2", writes=[r0])
        P.dma("sp", tdsb[:, 0:8] if False else dcol[:], tdsb, "ldc2", writes=[dcol])
        P.dma("sp", tmvs[:], tmv_d, "ldc2", writes=[tmvs])
        P.dma("sp", tmrs[:], tmr_d, "ldc2", writes=[tmrs])
        P.dma("sp", fcol[:], ffl, "ldc2", writes=[fcol])

        P.op("act", lambda s: s.activation(lnd[:], dcol[:], AF.Exp),
             reads=[dcol], writes=[lnd], tag="a1_e1")
        P.op("dve", lambda v: v.tensor_scalar(lnd[:], lnd[:], -1.0, None, ALU.mult),
             reads=[lnd], writes=[lnd], tag="a1_neg")
        P.op("act", lambda s: s.activation(dcol[:], lnd[:], AF.Exp),
             reads=[lnd], writes=[dcol], tag="a1_d")

        emit_ln_stats(E, xin, tag="a1")

        for c in range(DC):
            emit_ln_norm(E, xin, xhc, tag=f"a1c{c}", chunks=[c], dst_stride=0)
            psl = slice(c * TPC, c * TPC + TPC)
            P.op("dve", lambda v, psl=psl: v.memset(prev[:, psl.start:psl.start + 1], 0.0),
                 writes=[prev], tag=f"a1_pz{c}")
            P.op("dve", lambda v, c=c, psl=psl: v.tensor_tensor_scan(
                prev[:, psl.start + 1:psl.stop],
                dcol[:, c:c + 1].to_broadcast([128, TPC - 1]),
                xhc[:, 0:TPC - 1], 0.0, ALU.mult, ALU.add),
                 reads=[dcol, xhc], writes=[prev], tag=f"a1_scan{c}")
            P.op("dve", lambda v, c=c, psl=psl: v.scalar_tensor_tensor(
                slast[:, c:c + 1], prev[:, psl.stop - 1:psl.stop], dcol[:, c:c + 1],
                xhc[:, TPC - 1:TPC], ALU.mult, ALU.add),
                 reads=[prev, dcol, xhc], writes=[slast], tag=f"a1_sl{c}")

        P.dma("pool", cdr, slast[:], "cw", reads=[slast], writes=["cdr"], tag="a1_cw")
        P.collective("AllGather", ALU.bypass, [cdr], [agc],
                     reads=["cdr"], writes=["agc"], tag="a1_ag")
        P.dma("sp", carry[:], agc.rearrange("(r p) c -> r p c", r=2)[0], "cr",
              reads=["agc"], writes=[carry], tag="a1_cr")
        P.op("dve", lambda v: v.tensor_scalar(carry[:], carry[:], fcol[:, 0:1], None,
                                              ALU.mult),
             reads=[carry, fcol], writes=[carry], tag="a1_cf")

        for c in range(DC):
            psl = slice(c * TPC, (c + 1) * TPC)
            P.op("act", lambda s, c=c: s.activation(t3[:], r0[:, 0:TPC], AF.Exp,
                                                    scale=lnd[:, c:c + 1]),
                 reads=[r0, lnd], writes=[t3], tag=f"a1_t3{c}")
            P.op("dve", lambda v, c=c, psl=psl: v.scalar_tensor_tensor(
                prev[:, psl], t3[:], carry[:, c:c + 1], prev[:, psl],
                ALU.mult, ALU.add),
                 reads=[t3, carry, prev], writes=[prev], tag=f"a1_corr{c}")

        xin_r = xin.rearrange("(c p) t -> p c t", p=128)
        xout_r = xout.rearrange("(c p) t -> p c t", p=128)
        for tb in range(4):
            sl = slice(tb * 512, (tb + 1) * 512)
            for c in range(DC):
                slot = c % 2
                csl = slice(c * 512, (c + 1) * 512)
                psl = slice(c * TPC + tb * 512, c * TPC + (tb + 1) * 512)
                P.dma("sp", outc[slot][:, 0:512], xin_r[:, c, sl], f"lxa_{slot}",
                      writes=[outc[slot]], tag=f"a1_lx{tb}_{c}")
                P.op("dve", lambda v, slot=slot, sl=sl: v.tensor_tensor(
                    dd[:], outc[slot][:, 0:512], E.mb[:, sl], ALU.subtract),
                     reads=[outc[slot], E.mb], writes=[dd], tag=f"a1_s{tb}_{c}")
                P.op("dve", lambda v, sl=sl: v.tensor_tensor(
                    dd[:], dd[:], E.rb[:, sl], ALU.mult),
                     reads=[dd, E.rb], writes=[dd], tag=f"a1_m{tb}_{c}")
                P.op("dve", lambda v, psl=psl: v.tensor_tensor(
                    dd[:], dd[:], prev[:, psl], ALU.subtract),
                     reads=[dd, prev], writes=[dd], tag=f"a1_dd{tb}_{c}")
                P.op("dve", lambda v, c=c, csl=csl, psl=psl: v.scalar_tensor_tensor(
                    vin[:, csl], dd[:], tmvs[:, c:c + 1], prev[:, psl],
                    ALU.mult, ALU.add),
                     reads=[dd, tmvs, prev], writes=[vin], tag=f"a1_vin{tb}_{c}")
                P.op("dve", lambda v, c=c, csl=csl, psl=psl: v.scalar_tensor_tensor(
                    rin[:, csl], dd[:], tmrs[:, c:c + 1], prev[:, psl],
                    ALU.mult, ALU.add),
                     reads=[dd, tmrs, prev], writes=[rin], tag=f"a1_rin{tb}_{c}")
            for (wd, src_sb, dst_sb, act, nm) in (
                    (wvt_d, vin, v_sb, None, "v"), (wrt_d, rin, r_sb, "sig", "r")):
                for quarter in range(4):
                    P.dma("sp", wsl[:].rearrange("p (c f) -> p c f", c=DC),
                          wd.rearrange("(c p) f -> p c f", p=128)[:, :, quarter * 256:(quarter + 1) * 256],
                          "lw", writes=[wsl], tag=f"a1_lw{nm}{tb}_{quarter}")
                    for oc in range(2):
                        o = quarter * 2 + oc
                        pslot = oc % 2
                        for dc in range(DC):
                            P.op("pe", lambda t, dc=dc, oc=oc, pslot=pslot,
                                 src_sb=src_sb: t.matmul(
                                ps_p[pslot][:], wsl[:, dc * 256 + oc * 128:dc * 256 + oc * 128 + 128],
                                src_sb[:, dc * 512:(dc + 1) * 512],
                                start=(dc == 0), stop=(dc == DC - 1)),
                                 reads=[wsl, src_sb], writes=[ps_p[pslot]],
                                 psum_accum=(dc > 0), tag=f"a1_p{nm}{tb}_{o}_{dc}")
                        if act == "sig":
                            P.op("act", lambda s, o=o, pslot=pslot, dst_sb=dst_sb:
                                 s.activation(
                                dst_sb[:, o * 512:(o + 1) * 512], ps_p[pslot][:],
                                AF.Sigmoid),
                                 reads=[ps_p[pslot]], writes=[dst_sb],
                                 tag=f"a1_e{nm}{tb}_{o}")
                        else:
                            P.op("act", lambda s, o=o, pslot=pslot, dst_sb=dst_sb:
                                 s.copy(
                                dst_sb[:, o * 512:(o + 1) * 512], ps_p[pslot][:]),
                                 reads=[ps_p[pslot]], writes=[dst_sb],
                                 tag=f"a1_e{nm}{tb}_{o}")
            for quarter in range(4):
                P.dma("sp", wsl[:].rearrange("p (c f) -> p c f", c=DC),
                      wot_d.rearrange("(c p) f -> p c f", p=128)[:, :, quarter * 256:(quarter + 1) * 256],
                      "lw", writes=[wsl], tag=f"a1_lwo{tb}_{quarter}")
                for oc in range(2):
                    o = quarter * 2 + oc
                    pslot = oc % 2
                    cslot = oc % 2
                    for dc in range(DC):
                        P.op("pe", lambda t, dc=dc, oc=oc, pslot=pslot: t.matmul(
                            ps_p[pslot][:], wsl[:, dc * 256 + oc * 128:dc * 256 + oc * 128 + 128],
                            v_sb[:, dc * 512:(dc + 1) * 512],
                            start=(dc == 0), stop=(dc == DC - 1)),
                             reads=[wsl, v_sb], writes=[ps_p[pslot]],
                             psum_accum=(dc > 0), tag=f"a1_po{tb}_{o}_{dc}")
                    P.op("dve", lambda v, o=o, pslot=pslot, cslot=cslot: v.tensor_tensor(
                        tcmb[cslot][:], ps_p[pslot][:], r_sb[:, o * 512:(o + 1) * 512],
                        ALU.mult),
                         reads=[ps_p[pslot], r_sb], writes=[tcmb[cslot]],
                         tag=f"a1_rv{tb}_{o}")
                    P.dma("sp", outc[cslot][:, 0:512], xin_r[:, o, sl], f"lxf_{cslot}",
                          writes=[outc[cslot]], tag=f"a1_lxf{tb}_{o}")
                    P.op("dve", lambda v, cslot=cslot: v.tensor_tensor(
                        tcmb[cslot][:], tcmb[cslot][:], outc[cslot][:, 0:512], ALU.add),
                         reads=[tcmb[cslot], outc[cslot]], writes=[tcmb[cslot]],
                         tag=f"a1_fin{tb}_{o}")
                    P.dma("pool", xout_r[:, o, sl], tcmb[cslot][:], f"sx_{cslot}",
                          reads=[tcmb[cslot]], writes=["xout"], tag=f"a1_st{tb}_{o}")

        P.wait_all_stores("pool")
        P.finalize(block)
    return nc




SC = 256
NSC = TPC // SC
CH = 64
NPAIR = 10


def build_attn0():
    nc = bass.Bass("TRN2", target_bir_lowering=False)
    E = Env()
    E.nc = nc
    xin = nc.dram_tensor("xin", [D, TPC], F32, kind="ExternalInput").ap()
    wwt_d = nc.dram_tensor("wwt", [D, D], F32, kind="ExternalInput").ap()
    wrt_d = nc.dram_tensor("wrt", [D, D], F32, kind="ExternalInput").ap()
    dec_d = nc.dram_tensor("dec", [1, 1], F32, kind="ExternalInput").ap()
    lga_d = nc.dram_tensor("lga", [1, 1], F32, kind="ExternalInput").ap()
    ffl = nc.dram_tensor("ffl", [128, 1], F32, kind="ExternalInput").ap()
    r0_d = nc.dram_tensor("c_r0", [128, TPC], F32, kind="ExternalInput").ap()
    r64_d = nc.dram_tensor("c_r64", [64, 64], F32, kind="ExternalInput").ap()
    caus_d = nc.dram_tensor("c_caus", [64, 64], F32, kind="ExternalInput").ap()
    r255_d = nc.dram_tensor("c_r255", [128, 2], F32, kind="ExternalInput").ap()
    xout = nc.dram_tensor("xout", [D, TPC], F32, kind="ExternalOutput").ap()
    reads_d = nc.dram_tensor("reads_d", [D, TPC], F32, kind="Internal").ap()
    bnd_d = nc.dram_tensor("bnd_d", [128, 8], F32, kind="Internal").ap()
    agb = nc.dram_tensor("agb", [256, 8], F32, kind="Internal").ap()
    wt_d = nc.dram_tensor("wt_d", [128, DC * D], F32, kind="Internal").ap()
    agw = nc.dram_tensor("agw", [256, DC * D], F32, kind="Internal").ap()

    with ExitStack() as ctx:
        ent = ctx.enter_context
        P = Prog(nc, ctx)
        E.P = P
        alloc_common(nc, ctx, E)
        xh = ent(nc.sbuf_tensor("xh", [128, DC * TPC], F32))
        wt_sb = ent(nc.sbuf_tensor("wt_sb", [128, DC * D], F32))
        gf = ent(nc.sbuf_tensor("gf", [128, TPC], F32))
        scl = ent(nc.sbuf_tensor("scl", [1, 8], F32))
        lgb = ent(nc.sbuf_tensor("lgb", [128, 1], F32))
        gcb = ent(nc.sbuf_tensor("gcb", [128, 1], F32))
        alb = ent(nc.sbuf_tensor("alb", [128, 1], F32))
        vsa = ent(nc.sbuf_tensor("vsa", [128, 2], F32))
        fcol = ent(nc.sbuf_tensor("fcol", [128, 1], F32))
        bndc = ent(nc.sbuf_tensor("bndc", [128, 8], F32))
        masks = ent(nc.sbuf_tensor("masks", [64, 4 * 64], F32))
        mscr = ent(nc.sbuf_tensor("mscr", [64, 64], F32))
        ps_bc1 = ent(nc.psum_tensor("ps_bc1", [128, 512], F32))
        block = ent(nc.Block())

        load_common(E)
        P.dma("sp", gf[:], r0_d, "ldc2", writes=[gf])
        P.dma("sp", scl[:, 0:1], dec_d, "ldc2", writes=[scl])
        P.dma("sp", scl[:, 1:2], lga_d, "ldc2", writes=[scl])
        P.dma("sp", fcol[:], ffl, "ldc2", writes=[fcol])
        P.dma("sp", mscr[:], r64_d, "ldc3", writes=[mscr])
        P.dma("sp", vsa[:], r255_d, "ldc2", writes=[vsa])

        P.op("act", lambda s: s.activation(scl[:, 2:3], scl[:, 0:1], AF.Sigmoid),
             reads=[scl], writes=[scl], tag="a0_sg")
        P.op("act", lambda s: s.activation(scl[:, 3:4], scl[:, 2:3], AF.Ln),
             reads=[scl], writes=[scl], tag="a0_lng")
        P.op("act", lambda s: s.activation(scl[:, 4:5], scl[:, 1:2], AF.Exp),
             reads=[scl], writes=[scl], tag="a0_al")
        for (col, dst, nm) in ((3, lgb, "lg"), (4, alb, "al")):
            P.op("pe", lambda t, col=col, dst=dst: t.matmul(
                ps_bc1[:, 0:1], E.ones_r_sb[:], scl[:, col:col + 1],
                start=True, stop=True),
                 reads=[E.ones_r_sb, scl], writes=[ps_bc1], tag=f"a0_bc{nm}")
            P.op("act", lambda s, dst=dst: s.copy(dst[:], ps_bc1[:, 0:1]),
                 reads=[ps_bc1], writes=[dst], tag=f"a0_bce{nm}")
        P.op("dve", lambda v: v.tensor_scalar(gcb[:], lgb[:], 256.0, None, ALU.mult),
             reads=[lgb], writes=[gcb], tag="a0_g256")
        P.op("act", lambda s: s.activation(gcb[:], gcb[:], AF.Exp),
             reads=[gcb], writes=[gcb], tag="a0_g256e")
        P.op("dve", lambda v: v.tensor_scalar(vsa[:], vsa[:], lgb[:, 0:1], None,
                                              ALU.mult),
             reads=[vsa, lgb], writes=[vsa], tag="a0_vs")
        P.op("act", lambda s: s.activation(vsa[:], vsa[:], AF.Exp),
             reads=[vsa], writes=[vsa], tag="a0_vse")
        P.op("dve", lambda v: v.tensor_scalar(gf[:], gf[:], lgb[:, 0:1], None,
                                              ALU.mult),
             reads=[gf, lgb], writes=[gf], tag="a0_gf")
        P.op("act", lambda s: s.activation(gf[:], gf[:], AF.Exp),
             reads=[gf], writes=[gf], tag="a0_gfe")
        for dlt in range(4):
            P.op("dve", lambda v, dlt=dlt: v.tensor_scalar(
                masks[:, dlt * 64:(dlt + 1) * 64], mscr[:], float(64 * dlt),
                lgb[0:64, 0:1], ALU.add, ALU.mult),
                 reads=[mscr, lgb], writes=[masks], tag=f"a0_m{dlt}")
            P.op("dve", lambda v, dlt=dlt: v.tensor_scalar(
                masks[:, dlt * 64:(dlt + 1) * 64], masks[:, dlt * 64:(dlt + 1) * 64],
                0.0, None, ALU.min),
                 reads=[masks], writes=[masks], tag=f"a0_mc{dlt}")
            P.op("act", lambda s, dlt=dlt: s.activation(
                masks[:, dlt * 64:(dlt + 1) * 64], masks[:, dlt * 64:(dlt + 1) * 64],
                AF.Exp),
                 reads=[masks], writes=[masks], tag=f"a0_me{dlt}")
        P.dma("sp", mscr[:], caus_d, "ldc3", writes=[mscr], tag="a0_ldcaus")
        P.op("dve", lambda v: v.tensor_tensor(
            masks[:, 0:64], masks[:, 0:64], mscr[:], ALU.mult),
             reads=[masks, mscr], writes=[masks], tag="a0_m0c")

        for dh in range(16):
            P.op("dve", lambda v, dh=dh: v.memset(wt_sb[:, dh * 512:(dh + 1) * 512], 0.0),
                 writes=[wt_sb], tag=f"a0_wz{dh}")

        with ExitStack() as scA:
            alloc_ln(nc, scA, E)
            emit_ln(E, xin, xh, tag="a0")
            for c in range(DC):
                P.op("dve", lambda v, c=c: v.tensor_copy(
                    bndc[:, c:c + 1], xh[:, c * TPC + TPC - 1:c * TPC + TPC]),
                     reads=[xh], writes=[bndc], tag=f"a0_bnd{c}")
            P.dma("pool", bnd_d, bndc[:], "bw", reads=[bndc], writes=["bnd_d"],
                  tag="a0_bw")
            P.collective("AllGather", ALU.bypass, [bnd_d], [agb],
                         reads=["bnd_d"], writes=["agb"], tag="a0_agb")
            P.dma("sp", bndc[:], agb.rearrange("(r p) c -> r p c", r=2)[0], "br",
                  reads=["agb"], writes=[bndc], tag="a0_br")
            P.op("dve", lambda v: v.tensor_scalar(bndc[:], bndc[:], fcol[:, 0:1],
                                                  None, ALU.mult),
                 reads=[bndc, fcol], writes=[bndc], tag="a0_bf")
            P.barrier("a0_postln")

        with ExitStack() as scB:
            entB = scB.enter_context
            wwt = entB(nc.sbuf_tensor("wwt_sb", [128, DC * D], F32))
            v_tm = entB(nc.sbuf_tensor("v_tm", [128, 2 * D], F32))
            v64 = entB(nc.sbuf_tensor("v64", [64, 4 * D], F32))
            vsc = entB(nc.sbuf_tensor("vsc", [128, 2 * D], F32))
            wkf = entB(nc.sbuf_tensor("wkf", [128, DC * SC], F32))
            wkt = entB(nc.sbuf_tensor("wkt", [128, 2 * D], F32))
            rks = entB(nc.sbuf_tensor("rks", [128, DC * SC], F32))
            asb = entB(nc.sbuf_tensor("asb", [64, NPAIR * 64], F32))
            rde = entB(nc.sbuf_tensor("rde", [128, SC], F32))
            ps_v = [entB(nc.psum_tensor(f"ps_v{i}", [128, 512], F32)) for i in range(2)]
            ps_s = [entB(nc.psum_tensor(f"ps_s{i}", [64, 512], F32)) for i in range(2)]
            ps_r = [entB(nc.psum_tensor(f"ps_r{i}", [128, 512], F32)) for i in range(2)]
            ps_t = entB(nc.psum_tensor("ps_t", [128, 512], F32))
            P.dma("sp", wwt[:].rearrange("p (c f) -> p c f", c=DC),
                  wwt_d.rearrange("(c p) f -> p c f", p=128), "lww", writes=[wwt],
                  tag="a0_lww")
            pairs = [(cq, ck) for cq in range(4) for ck in range(cq + 1)]
            for s in range(NSC):
                S0 = s * SC
                for tt in range(2):
                    for eh in range(2):
                        pslot = (tt * 2 + eh) % 2
                        for dc in range(DC):
                            P.op("pe", lambda t, tt=tt, eh=eh, dc=dc, S0=S0,
                                 pslot=pslot: t.matmul(
                                     ps_v[pslot][:],
                                     xh[:, dc * TPC + S0 + tt * 128:
                                        dc * TPC + S0 + tt * 128 + 128],
                                     wwt[:, dc * D + eh * 512:dc * D + eh * 512 + 512],
                                     start=(dc == 0), stop=(dc == DC - 1)),
                                 reads=[xh, wwt], writes=[ps_v[pslot]],
                                 psum_accum=(dc > 0), tag=f"a0_v{s}_{tt}_{eh}_{dc}")
                        P.op("act", lambda a, tt=tt, eh=eh, pslot=pslot: a.copy(
                            v_tm[:, tt * D + eh * 512:tt * D + eh * 512 + 512],
                            ps_v[pslot][:]),
                             reads=[ps_v[pslot]], writes=[v_tm],
                             tag=f"a0_ve{s}_{tt}_{eh}")
                        for hh in range(2):
                            P.op("dve", lambda v, tt=tt, eh=eh, hh=hh: v.tensor_copy(
                                v64[0:64, (tt * 2 + hh) * D + eh * 512:
                                    (tt * 2 + hh) * D + eh * 512 + 512],
                                v_tm[hh * 64:hh * 64 + 64,
                                     tt * D + eh * 512:tt * D + eh * 512 + 512]),
                                 reads=[v_tm], writes=[v64],
                                 tag=f"a0_v64{s}_{tt}_{eh}_{hh}")
                for dc in range(DC):
                    if s == 0:
                        P.op("dve", lambda v, dc=dc: v.tensor_copy(
                            wkf[:, dc * SC + 1:(dc + 1) * SC],
                            xh[:, dc * TPC:dc * TPC + SC - 1]),
                             reads=[xh], writes=[wkf], tag=f"a0_wk{s}_{dc}")
                        P.op("dve", lambda v, dc=dc: v.tensor_copy(
                            wkf[:, dc * SC:dc * SC + 1], bndc[:, dc:dc + 1]),
                             reads=[bndc], writes=[wkf], tag=f"a0_wkb{s}_{dc}")
                    else:
                        P.op("dve", lambda v, dc=dc, S0=S0: v.tensor_copy(
                            wkf[:, dc * SC:(dc + 1) * SC],
                            xh[:, dc * TPC + S0 - 1:dc * TPC + S0 + SC - 1]),
                             reads=[xh], writes=[wkf], tag=f"a0_wk{s}_{dc}")
                for tt in range(2):
                    for dc in range(DC):
                        P.op("pe", lambda t, tt=tt, dc=dc: t.transpose(
                            ps_t[:, 0:128],
                            wkf[:, dc * SC + tt * 128:dc * SC + tt * 128 + 128],
                            E.ident_sb[:]),
                             reads=[wkf, E.ident_sb], writes=[ps_t],
                             tag=f"a0_tr{s}_{tt}_{dc}")
                        P.op("act", lambda a, tt=tt, dc=dc: a.copy(
                            wkt[:, tt * D + dc * 128:tt * D + dc * 128 + 128],
                            ps_t[:, 0:128]),
                             reads=[ps_t], writes=[wkt], tag=f"a0_tre{s}_{tt}_{dc}")
                for tt in range(2):
                    P.op("dve", lambda v, tt=tt: v.tensor_scalar(
                        vsc[:, tt * D:(tt + 1) * D], v_tm[:, tt * D:(tt + 1) * D],
                        vsa[:, tt:tt + 1], None, ALU.mult),
                         reads=[v_tm, vsa], writes=[vsc], tag=f"a0_vsc{s}_{tt}")
                for dc in range(DC):
                    P.op("dve", lambda v, dc=dc, S0=S0: v.tensor_tensor(
                        rks[:, dc * SC:(dc + 1) * SC],
                        xh[:, dc * TPC + S0:dc * TPC + S0 + SC], gf[:, 0:SC],
                        ALU.mult),
                         reads=[xh, gf], writes=[rks], tag=f"a0_rks{s}_{dc}")
                for pi, (cq, ck) in enumerate(pairs):
                    pslot = pi % 2
                    for dc in range(DC):
                        P.op("pe", lambda t, pi=pi, cq=cq, ck=ck, dc=dc, S0=S0,
                             pslot=pslot: t.matmul(
                                 ps_s[pslot][:, 0:64],
                                 wkf[:, dc * SC + ck * 64:dc * SC + ck * 64 + 64],
                                 xh[:, dc * TPC + S0 + cq * 64:
                                    dc * TPC + S0 + cq * 64 + 64],
                                 start=(dc == 0), stop=(dc == DC - 1)),
                             reads=[wkf, xh], writes=[ps_s[pslot]],
                             psum_accum=(dc > 0), tag=f"a0_s{s}_{pi}_{dc}")
                    P.op("dve", lambda v, pi=pi, cq=cq, ck=ck, pslot=pslot:
                         v.tensor_tensor(
                             asb[:, pi * 64:(pi + 1) * 64], ps_s[pslot][:, 0:64],
                             masks[:, (cq - ck) * 64:(cq - ck) * 64 + 64], ALU.mult),
                         reads=[ps_s[pslot], masks], writes=[asb],
                         tag=f"a0_am{s}_{pi}")
                for dc in range(DC):
                    pslot = dc % 2
                    first = True
                    for ec in range(DC):
                        P.op("pe", lambda t, dc=dc, ec=ec, pslot=pslot, f=first:
                             t.matmul(
                                 ps_r[pslot][:, 0:SC],
                                 wt_sb[:, ec * D + dc * 128:ec * D + dc * 128 + 128],
                                 rks[:, ec * SC:(ec + 1) * SC],
                                 start=f, stop=False),
                             reads=[wt_sb, rks], writes=[ps_r[pslot]],
                             psum_accum=(not first), tag=f"a0_ri{s}_{dc}_{ec}")
                        first = False
                    for pi, (cq, ck) in enumerate(pairs):
                        last = (pi == NPAIR - 1)
                        P.op("pe", lambda t, dc=dc, pi=pi, cq=cq, ck=ck, pslot=pslot,
                             last=last: t.matmul(
                                 ps_r[pslot][:, cq * 64:cq * 64 + 64],
                                 v64[0:64, ck * D + dc * 128:ck * D + dc * 128 + 128],
                                 asb[0:64, pi * 64:(pi + 1) * 64],
                                 start=False, stop=last),
                             reads=[v64, asb], writes=[ps_r[pslot]],
                             psum_accum=True, tag=f"a0_ra{s}_{dc}_{pi}")
                    P.op("act", lambda a, dc=dc, pslot=pslot: a.copy(
                        rde[:], ps_r[pslot][:, 0:SC]),
                         reads=[ps_r[pslot]], writes=[rde], tag=f"a0_re{s}_{dc}")
                    P.dma("pool", reads_d.rearrange("(c p) t -> p c t", p=128)
                          [:, dc, S0:S0 + SC], rde[:], "rdw", reads=[rde],
                          writes=["reads_d"], tag=f"a0_rw{s}_{dc}")
                for ec in range(DC):
                    for dh in range(2):
                        pslot = (ec * 2 + dh) % 2
                        for tt in range(2):
                            P.op("pe", lambda t, ec=ec, dh=dh, tt=tt, pslot=pslot:
                                 t.matmul(
                                     ps_v[pslot][:],
                                     wkt[:, tt * D + ec * 128:tt * D + ec * 128 + 128],
                                     vsc[:, tt * D + dh * 512:tt * D + dh * 512 + 512],
                                     start=(tt == 0), stop=(tt == 1)),
                                 reads=[wkt, vsc], writes=[ps_v[pslot]],
                                 psum_accum=(tt > 0), tag=f"a0_u{s}_{ec}_{dh}_{tt}")
                        P.op("dve", lambda v, ec=ec, dh=dh, pslot=pslot:
                             v.scalar_tensor_tensor(
                                 wt_sb[:, ec * D + dh * 512:ec * D + dh * 512 + 512],
                                 wt_sb[:, ec * D + dh * 512:ec * D + dh * 512 + 512],
                                 gcb[:, 0:1], ps_v[pslot][:], ALU.mult, ALU.add),
                             reads=[wt_sb, gcb, ps_v[pslot]], writes=[wt_sb],
                             tag=f"a0_uf{s}_{ec}_{dh}")
            P.dma("pool", wt_d, wt_sb[:], "ww", reads=[wt_sb], writes=["wt_d"],
                  tag="a0_ww")
            P.collective("AllGather", ALU.bypass, [wt_d], [agw],
                         reads=["wt_d"], writes=["agw"], tag="a0_agw")
            P.dma("sp", wt_sb[:], agw.rearrange("(r p) c -> r p c", r=2)[0], "wr",
                  reads=["agw"], writes=[wt_sb], tag="a0_wr")
            P.barrier("a0_postsc")

        with ExitStack() as scC:
            entC = scC.enter_context
            rks2 = entC(nc.sbuf_tensor("rks2", [128, DC * 512], F32))
            crd = entC(nc.sbuf_tensor("crd", [128, DC * 512], F32))
            rdld = entC(nc.sbuf_tensor("rdld", [128, DC * 512], F32))
            wrl = entC(nc.sbuf_tensor("wrl", [128, DC * 256], F32))
            t1f = [entC(nc.sbuf_tensor(f"t1f{i}", [128, 512], F32)) for i in range(2)]
            oc2 = [entC(nc.sbuf_tensor(f"oc2{i}", [128, 512], F32)) for i in range(2)]
            ps_c = [entC(nc.psum_tensor(f"ps_c{i}", [128, 512], F32)) for i in range(2)]
            P.op("dve", lambda v: v.tensor_scalar(gf[:], gf[:], fcol[:, 0:1], None,
                                                  ALU.mult),
                 reads=[gf, fcol], writes=[gf], tag="a0_gff")
            xin_r = xin.rearrange("(c p) t -> p c t", p=128)
            xout_r = xout.rearrange("(c p) t -> p c t", p=128)
            reads_r = reads_d.rearrange("(c p) t -> p c t", p=128)
            for tb in range(4):
                sl = slice(tb * 512, (tb + 1) * 512)
                for dc in range(DC):
                    P.op("dve", lambda v, dc=dc, sl=sl: v.tensor_tensor(
                        rks2[:, dc * 512:(dc + 1) * 512],
                        xh[:, dc * TPC + sl.start:dc * TPC + sl.stop], gf[:, sl],
                        ALU.mult),
                         reads=[xh, gf], writes=[rks2], tag=f"a0_k2{tb}_{dc}")
                    P.dma("sp", rdld[:, dc * 512:(dc + 1) * 512], reads_r[:, dc, sl],
                          "rdr", reads=["reads_d"], writes=[rdld],
                          tag=f"a0_rl{tb}_{dc}")
                for dc in range(DC):
                    pslot = dc % 2
                    for ec in range(DC):
                        P.op("pe", lambda t, dc=dc, ec=ec, pslot=pslot: t.matmul(
                            ps_c[pslot][:],
                            wt_sb[:, ec * D + dc * 128:ec * D + dc * 128 + 128],
                            rks2[:, ec * 512:(ec + 1) * 512],
                            start=(ec == 0), stop=(ec == DC - 1)),
                             reads=[wt_sb, rks2], writes=[ps_c[pslot]],
                             psum_accum=(ec > 0), tag=f"a0_c{tb}_{dc}_{ec}")
                    P.op("act", lambda a, dc=dc, pslot=pslot: a.copy(
                        crd[:, dc * 512:(dc + 1) * 512], ps_c[pslot][:]),
                         reads=[ps_c[pslot]], writes=[crd], tag=f"a0_ce{tb}_{dc}")
                for half in range(4):
                    P.dma("sp", wrl[:].rearrange("p (c f) -> p c f", c=DC),
                          wrt_d.rearrange("(c p) f -> p c f", p=128)
                          [:, :, half * 256:(half + 1) * 256],
                          "lwr", writes=[wrl], tag=f"a0_lwr{tb}_{half}")
                    for oc in range(2):
                        e2 = half * 2 + oc
                        pslot = oc % 2
                        cslot = oc % 2
                        for dc in range(DC):
                            P.op("pe", lambda t, dc=dc, oc=oc, pslot=pslot: t.matmul(
                                ps_c[pslot][:],
                                wrl[:, dc * 256 + oc * 128:dc * 256 + oc * 128 + 128],
                                rdld[:, dc * 512:(dc + 1) * 512],
                                start=(dc == 0), stop=False),
                                 reads=[wrl, rdld], writes=[ps_c[pslot]],
                                 psum_accum=(dc > 0), tag=f"a0_o{tb}_{e2}_{dc}")
                        for dc in range(DC):
                            P.op("pe", lambda t, dc=dc, oc=oc, pslot=pslot: t.matmul(
                                ps_c[pslot][:],
                                wrl[:, dc * 256 + oc * 128:dc * 256 + oc * 128 + 128],
                                crd[:, dc * 512:(dc + 1) * 512],
                                start=False, stop=(dc == DC - 1)),
                                 reads=[wrl, crd], writes=[ps_c[pslot]],
                                 psum_accum=True, tag=f"a0_oc{tb}_{e2}_{dc}")
                        P.dma("sp", oc2[cslot][:, 0:512], xin_r[:, e2, sl],
                              f"lx2_{cslot}",
                              writes=[oc2[cslot]], tag=f"a0_lx{tb}_{e2}")
                        P.op("dve", lambda v, e2=e2, sl=sl, cslot=cslot:
                             v.tensor_tensor(
                                 t1f[cslot][:], oc2[cslot][:, 0:512],
                                 xh[:, e2 * TPC + sl.start:e2 * TPC + sl.stop],
                                 ALU.add),
                             reads=[oc2[cslot], xh], writes=[t1f[cslot]],
                             tag=f"a0_xx{tb}_{e2}")
                        P.op("dve", lambda v, cslot=cslot, pslot=pslot:
                             v.scalar_tensor_tensor(
                                 t1f[cslot][:], ps_c[pslot][:], alb[:, 0:1],
                                 t1f[cslot][:], ALU.mult, ALU.add),
                             reads=[ps_c[pslot], alb, t1f[cslot]],
                             writes=[t1f[cslot]], tag=f"a0_fin{tb}_{e2}")
                        P.dma("pool", xout_r[:, e2, sl], t1f[cslot][:],
                              f"sx2_{cslot}", reads=[t1f[cslot]], writes=["xout"],
                              tag=f"a0_st{tb}_{e2}")

        P.wait_all_stores("pool")
        P.finalize(block)
    return nc




BIGIDX = 1.0e9
DBG_FFN1 = False
F1_TRUNC = 0


def build_ffn1():
    nc = bass.Bass("TRN2", target_bir_lowering=False)
    E = Env()
    E.nc = nc
    xin = nc.dram_tensor("xin", [D, TPC], F32, kind="ExternalInput").ap()
    gwt_d = nc.dram_tensor("gwt", [D, 8], F32, kind="ExternalInput").ap()
    wg_d = nc.dram_tensor("wgT", [4, D, 4096], F32, kind="ExternalInput").ap()
    wu_d = nc.dram_tensor("wuT", [4, D, 4096], F32, kind="ExternalInput").ap()
    wd_d = nc.dram_tensor("wdT", [4, 4096, D], F32, kind="ExternalInput").ap()
    prm_d = nc.dram_tensor("prm", [8, 8], F32, kind="ExternalInput").ap()
    selm_d = nc.dram_tensor("selm", [128, 32], F32, kind="ExternalInput").ap()
    hoff_d = nc.dram_tensor("hoff", [128, 1], F32, kind="ExternalInput").ap()
    xout = nc.dram_tensor("xout", [D, TPC], F32, kind="ExternalOutput").ap()
    EXTW = 1040
    ext_d = nc.dram_tensor("ext_d", [TPC, EXTW], F32, kind="Internal").ap()
    age = nc.dram_tensor("age", [2 * TPC, EXTW], F32, kind="Internal").ap()
    xind = [nc.dram_tensor(f"xind{j}", [1024 + 128, EXTW], F32, kind="Internal").ap()
            for j in range(4)]
    acc = nc.dram_tensor("acc", [2 * TPC, D], F32, kind="Internal").ap()
    a_d = nc.dram_tensor("a_d", [128, 128], F32, kind="Internal").ap()
    aga = nc.dram_tensor("aga", [256, 128], F32, kind="Internal").ap()
    accr = nc.dram_tensor("accr", [2 * TPC, D], F32, kind="Internal").ap()

    with ExitStack() as ctx:
        ent = ctx.enter_context
        P = Prog(nc, ctx)
        E.P = P
        alloc_common(nc, ctx, E)
        dstu = ent(nc.sbuf_tensor("dstu", [128, 32 * 8], U32))
        utid = ent(nc.sbuf_tensor("utid", [128, 16], U32))
        utidf = ent(nc.sbuf_tensor("utidf", [128, 16], F32))
        prm_sb = ent(nc.sbuf_tensor("prm_sb", [8, 8], F32))
        selm_sb = ent(nc.sbuf_tensor("selm_sb", [128, 32], F32))
        hoffc = ent(nc.sbuf_tensor("hoffc", [128, 1], F32))
        ps_a = [ent(nc.psum_tensor(f"ps_a{i}", [128, 512], F32)) for i in range(2)]
        block = ent(nc.Block())

        load_common(E)
        P.dma("sp", prm_sb[:], prm_d, "ldc2", writes=[prm_sb])
        P.dma("sp", selm_sb[:], selm_d, "ldc2", writes=[selm_sb])
        P.dma("sp", hoffc[:], hoff_d, "ldc2", writes=[hoffc])

        P.op("pool", lambda g: g.iota(utidf[:], [[128, 16]], base=0,
                                      channel_multiplier=1,
                                      allow_small_or_imprecise_dtypes=True),
             writes=[utidf], tag="f1_iota")
        P.op("dve", lambda v: v.tensor_scalar(utidf[:], utidf[:], hoffc[:, 0:1],
                                              None, ALU.add),
             reads=[utidf, hoffc], writes=[utidf], tag="f1_tid")
        P.op("dve", lambda v: v.tensor_copy(utid[:], utidf[:]),
             reads=[utidf], writes=[utid], tag="f1_tidc")

        with ExitStack() as scA:
            entA = scA.enter_context
            xh = entA(nc.sbuf_tensor("xh", [128, DC * TPC], F32))
            a_tm = entA(nc.sbuf_tensor("a_tm", [128, 16 * 8], F32))
            ext_sb = [entA(nc.sbuf_tensor(f"ext_sb{i}", [128, EXTW], F32))
                      for i in range(2)]
            gwt_sb = entA(nc.sbuf_tensor("gwt_sb", [128, DC * 8], F32))
            alloc_ln(nc, scA, E)
            P.dma("sp", gwt_sb[:].rearrange("p (c n) -> p c n", c=DC),
                  gwt_d.rearrange("(c p) n -> p c n", p=128), "ldg", writes=[gwt_sb])
            for i in range(2):
                P.op("dve", lambda v, i=i: v.memset(ext_sb[i][:, 1033:EXTW], 0.0),
                     writes=[ext_sb[i]], tag=f"f1_pad{i}")
            emit_ln(E, xin, xh, tag="f1")
            for st in range(16):
                for dc in range(DC):
                    P.op("pe", lambda t, st=st, dc=dc: t.matmul(
                        ps_a[0][:, 0:8],
                        xh[:, dc * TPC + st * 128:dc * TPC + st * 128 + 128],
                        gwt_sb[:, dc * 8:(dc + 1) * 8],
                        start=(dc == 0), stop=(dc == DC - 1)),
                         reads=[xh, gwt_sb], writes=[ps_a[0]],
                         psum_accum=(dc > 0), tag=f"f1_lg{st}_{dc}")
                asl = slice(st * 8, (st + 1) * 8)
                P.op("act", lambda a, asl=asl: a.copy(a_tm[:, asl], ps_a[0][:, 0:8]),
                     reads=[ps_a[0]], writes=[a_tm], tag=f"f1_le{st}")
                sc0 = ext_sb[0]
                if F1_TRUNC == 2:
                    continue
                P.op("dve", lambda v, asl=asl, sc0=sc0: v.tensor_reduce(
                    sc0[:, 0:1], a_tm[:, asl], mybir.AxisListType.X, ALU.max),
                     reads=[a_tm], writes=[sc0], tag=f"f1_mx{st}")
                P.op("dve", lambda v, sc0=sc0: v.tensor_scalar(
                    sc0[:, 0:1], sc0[:, 0:1], -1.0, None, ALU.mult),
                     reads=[sc0], writes=[sc0], tag=f"f1_ng{st}")
                P.op("act", lambda a, asl=asl, sc0=sc0: a.activation(
                    a_tm[:, asl], a_tm[:, asl], AF.Exp, bias=sc0[:, 0:1]),
                     reads=[a_tm, sc0], writes=[a_tm], tag=f"f1_ex{st}")
                P.op("dve", lambda v, asl=asl, sc0=sc0: v.tensor_reduce(
                    sc0[:, 0:1], a_tm[:, asl], mybir.AxisListType.X, ALU.add),
                     reads=[a_tm], writes=[sc0], tag=f"f1_sm{st}")
                P.op("dve", lambda v, sc0=sc0: v.reciprocal(sc0[:, 1:2], sc0[:, 0:1]),
                     reads=[sc0], writes=[sc0], tag=f"f1_rc{st}")
                P.op("dve", lambda v, asl=asl, sc0=sc0: v.tensor_scalar(
                    a_tm[:, asl], a_tm[:, asl], sc0[:, 1:2], None, ALU.mult),
                     reads=[a_tm, sc0], writes=[a_tm], tag=f"f1_pr{st}")
            if F1_TRUNC == 1:
                dtr = nc.dram_tensor("dbg_atm", [128, 128], F32,
                                     kind="ExternalOutput").ap()
                P.dma("pool", dtr, a_tm[:], "dbgt", reads=[a_tm],
                      writes=["dbg_atm"], tag="f1_dbgt")
                P.wait_all_stores("pool")
                P.finalize(block)
                return nc
            for st in range(16):
                slot = st % 2
                esb = ext_sb[slot]
                for dc in range(DC):
                    pslot = dc % 2
                    P.op("pe", lambda t, st=st, dc=dc, pslot=pslot: t.transpose(
                        ps_a[pslot][:, 0:128],
                        xh[:, dc * TPC + st * 128:dc * TPC + st * 128 + 128],
                        E.ident_sb[:]),
                         reads=[xh, E.ident_sb], writes=[ps_a[pslot]],
                         tag=f"f1_tr{st}_{dc}")
                    P.op("act", lambda a, dc=dc, pslot=pslot, esb=esb: a.copy(
                        esb[:, dc * 128:(dc + 1) * 128], ps_a[pslot][:, 0:128]),
                         reads=[ps_a[pslot]], writes=[esb], tag=f"f1_te{st}_{dc}")
                P.op("dve", lambda v, st=st, esb=esb: v.tensor_copy(
                    esb[:, 1024:1025], utidf[:, st:st + 1]),
                     reads=[utidf], writes=[esb], tag=f"f1_tid{st}")
                P.op("dve", lambda v, st=st, esb=esb: v.tensor_copy(
                    esb[:, 1025:1033], a_tm[:, st * 8:(st + 1) * 8]),
                     reads=[a_tm], writes=[esb], tag=f"f1_aff{st}")
                P.dma("pool", ext_d.rearrange("(n p) c -> p n c", p=128)[:, st, :],
                      esb[:], f"exw_{slot}", reads=[esb], writes=["ext_d"],
                      tag=f"f1_exw{st}")
            acpy = entA(nc.sbuf_tensor("acpy", [128, 128], F32))
            P.op("dve", lambda v: v.tensor_copy(acpy[:], a_tm[:]),
                 reads=[a_tm], writes=[acpy], tag="f1_acpy")
            P.dma("pool", a_d, acpy[:], "adw", reads=[acpy], writes=["a_d"],
                  tag="f1_adw")
            if F1_TRUNC == 3:
                dtr = nc.dram_tensor("dbg_atm", [128, 128], F32,
                                     kind="ExternalOutput").ap()
                P.dma("pool", dtr, a_tm[:], "dbgt", reads=[a_tm],
                      writes=["dbg_atm"], tag="f1_dbgt")
                P.wait_all_stores("pool")
                P.finalize(block)
                return nc
            P.barrier("f1_p1")

        P.collective("AllGather", ALU.bypass, [ext_d], [age],
                     reads=["ext_d"], writes=["age"], tag="f1_ag")
        P.collective("AllGather", ALU.bypass, [a_d], [aga],
                     reads=["a_d"], writes=["aga"], tag="f1_aga")
        with ExitStack() as scB:
            entB = scB.enter_context
            at2 = entB(nc.sbuf_tensor("at2", [128, 32 * 8], F32))
            a_t = entB(nc.sbuf_tensor("a_t", [8, 2 * TPC], F32))
            a_p = entB(nc.sbuf_tensor("a_p", [8, 2 * TPC], F32))
            scr = entB(nc.sbuf_tensor("scr", [8, 2 * TPC], F32))
            eqm = entB(nc.sbuf_tensor("eqm", [8, 2 * TPC], F32))
            msk = entB(nc.sbuf_tensor("msk", [8, 2 * TPC], F32))
            iot = entB(nc.sbuf_tensor("iot", [8, 2 * TPC], F32))
            dstf = entB(nc.sbuf_tensor("dstf", [8, 2 * TPC], F32))
            s8 = entB(nc.sbuf_tensor("s8", [8, 16], F32))
            ps_b = entB(nc.psum_tensor("ps_b", [128, 512], F32))
            P.dma("sp", at2[:, 0:128], aga[0:128, :], "agr", reads=["aga"],
                  writes=[at2], tag="f1_agr0")
            P.dma("sp", at2[:, 128:256], aga[128:256, :], "agr", reads=["aga"],
                  writes=[at2], tag="f1_agr1")
            if F1_TRUNC == 4:
                dtr = nc.dram_tensor("dbg_atm", [128, 256], F32,
                                     kind="ExternalOutput").ap()
                P.dma("pool", dtr, at2[:], "dbgt", reads=[at2],
                      writes=["dbg_atm"], tag="f1_dbgt")
                P.wait_all_stores("pool")
                P.finalize(block)
                return nc
            for st in range(32):
                P.op("pe", lambda t, st=st: t.transpose(
                    ps_b[0:8, 0:128], at2[:, st * 8:(st + 1) * 8], E.ident_sb[:]),
                     reads=[at2, E.ident_sb], writes=[ps_b], tag=f"f1_t2{st}")
                P.op("act", lambda a, st=st: a.copy(
                    a_t[:, st * 128:(st + 1) * 128], ps_b[0:8, 0:128]),
                     reads=[ps_b], writes=[a_t], tag=f"f1_t2e{st}")
            for jj in range(8):
                P.op("pe", lambda t, jj=jj: t.matmul(
                    ps_b[0:8, 0:512], prm_sb[:], a_t[:, jj * 512:(jj + 1) * 512],
                    start=True, stop=True),
                     reads=[prm_sb, a_t], writes=[ps_b], tag=f"f1_pm{jj}")
                P.op("act", lambda a, jj=jj: a.copy(
                    a_p[:, jj * 512:(jj + 1) * 512], ps_b[0:8, 0:512]),
                     reads=[ps_b], writes=[a_p], tag=f"f1_pme{jj}")
            P.op("pool", lambda g: g.iota(iot[:], [[1, 2 * TPC]], base=0,
                                          channel_multiplier=0,
                                          allow_small_or_imprecise_dtypes=True),
                 writes=[iot], tag="f1_iot2")
            P.op("dve", lambda v: v.memset(s8[:, 0:1], 0.0), writes=[s8], tag="f1_l0")
            P.op("dve", lambda v: v.memset(s8[:, 1:2], 1.0), writes=[s8], tag="f1_h0")
            for r in range(40):
                P.op("dve", lambda v: v.tensor_tensor(s8[:, 2:3], s8[:, 0:1],
                                                      s8[:, 1:2], ALU.add),
                     reads=[s8], writes=[s8], tag=f"f1_b{r}a")
                P.op("dve", lambda v: v.tensor_scalar(s8[:, 2:3], s8[:, 2:3], 0.5,
                                                      None, ALU.mult),
                     reads=[s8], writes=[s8], tag=f"f1_b{r}b")
                P.op("dve", lambda v: v.tensor_scalar(scr[:], a_p[:], s8[:, 2:3],
                                                      None, ALU.is_gt, ALU.add,
                                                      accum_out=s8[:, 3:4]),
                     reads=[a_p, s8], writes=[scr, s8], tag=f"f1_b{r}c")
                P.op("dve", lambda v: v.tensor_scalar(s8[:, 4:5], s8[:, 3:4], 1024.0,
                                                      None, ALU.is_ge),
                     reads=[s8], writes=[s8], tag=f"f1_b{r}d")
                P.op("dve", lambda v: v.tensor_tensor(s8[:, 12:13], s8[:, 2:3],
                                                      s8[:, 0:1], ALU.subtract),
                     reads=[s8], writes=[s8], tag=f"f1_b{r}e1")
                P.op("dve", lambda v: v.scalar_tensor_tensor(
                    s8[:, 0:1], s8[:, 12:13], s8[:, 4:5], s8[:, 0:1],
                    ALU.mult, ALU.add),
                     reads=[s8], writes=[s8], tag=f"f1_b{r}e2")
                P.op("dve", lambda v: v.tensor_tensor(s8[:, 12:13], s8[:, 1:2],
                                                      s8[:, 2:3], ALU.subtract),
                     reads=[s8], writes=[s8], tag=f"f1_b{r}f1")
                P.op("dve", lambda v: v.scalar_tensor_tensor(
                    s8[:, 1:2], s8[:, 12:13], s8[:, 4:5], s8[:, 2:3],
                    ALU.mult, ALU.add),
                     reads=[s8], writes=[s8], tag=f"f1_b{r}f2")
            P.op("dve", lambda v: v.tensor_scalar(msk[:], a_p[:], s8[:, 1:2], None,
                                                  ALU.is_gt, ALU.add,
                                                  accum_out=s8[:, 5:6]),
                 reads=[a_p, s8], writes=[msk, s8], tag="f1_cgt")
            P.op("dve", lambda v: v.tensor_scalar(scr[:], a_p[:], s8[:, 0:1], None,
                                                  ALU.is_gt),
                 reads=[a_p, s8], writes=[scr], tag="f1_clo")
            P.op("dve", lambda v: v.tensor_tensor(eqm[:], scr[:], msk[:],
                                                  ALU.subtract),
                 reads=[scr, msk], writes=[eqm], tag="f1_eq")
            P.op("dve", lambda v: v.tensor_scalar(s8[:, 6:7], s8[:, 5:6], -1.0,
                                                  1024.0, ALU.mult, ALU.add),
                 reads=[s8], writes=[s8], tag="f1_tgt")
            P.op("dve", lambda v: v.memset(s8[:, 7:8], -1.0), writes=[s8], tag="f1_tl")
            P.op("dve", lambda v: v.memset(s8[:, 8:9], 4096.0), writes=[s8],
                 tag="f1_th")
            for r in range(14):
                P.op("dve", lambda v: v.tensor_tensor(s8[:, 9:10], s8[:, 7:8],
                                                      s8[:, 8:9], ALU.add),
                     reads=[s8], writes=[s8], tag=f"f1_i{r}a")
                P.op("dve", lambda v: v.tensor_scalar(s8[:, 9:10], s8[:, 9:10], 0.5,
                                                      None, ALU.mult),
                     reads=[s8], writes=[s8], tag=f"f1_i{r}b")
                P.op("dve", lambda v: v.scalar_tensor_tensor(
                    scr[:], iot[:], s8[:, 9:10], eqm[:], ALU.is_le, ALU.mult,
                    accum_out=s8[:, 10:11]),
                     reads=[iot, s8, eqm], writes=[scr, s8], tag=f"f1_i{r}c")
                P.op("dve", lambda v: v.tensor_scalar(s8[:, 11:12], s8[:, 10:11],
                                                      s8[:, 6:7], None, ALU.is_ge),
                     reads=[s8], writes=[s8], tag=f"f1_i{r}d")
                P.op("dve", lambda v: v.tensor_tensor(s8[:, 12:13], s8[:, 9:10],
                                                      s8[:, 8:9], ALU.subtract),
                     reads=[s8], writes=[s8], tag=f"f1_i{r}e1")
                P.op("dve", lambda v: v.scalar_tensor_tensor(
                    s8[:, 8:9], s8[:, 12:13], s8[:, 11:12], s8[:, 8:9],
                    ALU.mult, ALU.add),
                     reads=[s8], writes=[s8], tag=f"f1_i{r}e2")
                P.op("dve", lambda v: v.tensor_tensor(s8[:, 12:13], s8[:, 7:8],
                                                      s8[:, 9:10], ALU.subtract),
                     reads=[s8], writes=[s8], tag=f"f1_i{r}f1")
                P.op("dve", lambda v: v.scalar_tensor_tensor(
                    s8[:, 7:8], s8[:, 12:13], s8[:, 11:12], s8[:, 9:10],
                    ALU.mult, ALU.add),
                     reads=[s8], writes=[s8], tag=f"f1_i{r}f2")
            P.op("dve", lambda v: v.scalar_tensor_tensor(
                scr[:], iot[:], s8[:, 8:9], eqm[:], ALU.is_le, ALU.mult),
                 reads=[iot, s8, eqm], writes=[scr], tag="f1_fm1")
            P.op("dve", lambda v: v.tensor_tensor(msk[:], msk[:], scr[:], ALU.add),
                 reads=[msk, scr], writes=[msk], tag="f1_fm2")
            P.op("dve", lambda v: v.tensor_tensor_scan(
                scr[:], E.ones_c_sb[0:8, 0:1].to_broadcast([8, 2 * TPC]), msk[:],
                0.0, ALU.mult, ALU.add),
                 reads=[E.ones_c_sb, msk], writes=[scr], tag="f1_cs")
            P.op("dve", lambda v: v.tensor_scalar(scr[:], scr[:], -1.0, None,
                                                  ALU.add),
                 reads=[scr], writes=[scr], tag="f1_cs1")
            P.op("dve", lambda v: v.tensor_scalar(eqm[:], msk[:], -1.0, 1.0,
                                                  ALU.mult, ALU.add),
                 reads=[msk], writes=[eqm], tag="f1_inv")
            P.op("dve", lambda v: v.tensor_tensor(dstf[:], msk[:], scr[:], ALU.mult),
                 reads=[msk, scr], writes=[dstf], tag="f1_d1")
            P.op("dve", lambda v: v.scalar_tensor_tensor(
                scr[:], eqm[:], BIGIDX, dstf[:], ALU.mult, ALU.add),
                 reads=[eqm, dstf], writes=[scr], tag="f1_d2")
            dumc = entB(nc.sbuf_tensor("dumc", [128, 1], F32))
            P.op("pool", lambda g: g.iota(dumc[:], [[1, 1]], base=1024,
                                          channel_multiplier=1,
                                          allow_small_or_imprecise_dtypes=True),
                 writes=[dumc], tag="f1_dum")
            for st in range(32):
                P.op("pe", lambda t, st=st: t.transpose(
                    ps_b[0:128, 0:8], scr[0:8, st * 128:(st + 1) * 128],
                    E.ident_sb[0:8, 0:8]),
                     reads=[scr, E.ident_sb], writes=[ps_b], tag=f"f1_dt{st}")
                P.op("act", lambda a, st=st: a.copy(
                    dstu[:, st * 8:(st + 1) * 8].bitcast(F32), ps_b[0:128, 0:8]),
                     reads=[ps_b], writes=[dstu], tag=f"f1_dte{st}")
            P.op("dve", lambda v: v.tensor_scalar(
                dstu[:].bitcast(F32), dstu[:].bitcast(F32), dumc[:, 0:1], None,
                ALU.min),
                 reads=[dstu, dumc], writes=[dstu], tag="f1_dmin")
            P.op("dve", lambda v: v.tensor_copy(dstu[:], dstu[:].bitcast(F32)),
                 reads=[dstu], writes=[dstu], tag="f1_du")
            if DBG_FFN1:
                dbed = nc.dram_tensor("dbg_extd", [128, EXTW], F32,
                                      kind="ExternalOutput").ap()
                dbag = nc.dram_tensor("dbg_age", [128, EXTW], F32,
                                      kind="ExternalOutput").ap()
                dbag2 = nc.dram_tensor("dbg_age2", [128, EXTW], F32,
                                       kind="ExternalOutput").ap()
                with ExitStack() as scDb:
                    tdb = scDb.enter_context(nc.sbuf_tensor("tdb", [128, EXTW], F32))
                    P.dma("sp", tdb[:], ext_d[0:128, :], "dbr", reads=["ext_d"],
                          writes=[tdb], tag="dbg_e1")
                    P.dma("pool", dbed, tdb[:], "dbg", reads=[tdb], writes=["dbg_extd"])
                    P.dma("sp", tdb[:], age[0:128, :], "dbr", reads=["age"],
                          writes=[tdb], tag="dbg_e2")
                    P.dma("pool", dbag, tdb[:], "dbg", reads=[tdb], writes=["dbg_age"])
                    P.dma("sp", tdb[:], age[2048:2176, :], "dbr", reads=["age"],
                          writes=[tdb], tag="dbg_e3")
                    P.dma("pool", dbag2, tdb[:], "dbg", reads=[tdb], writes=["dbg_age2"])
                    P.barrier("dbg_eb")
                dbt = nc.dram_tensor("dbg_at2", [128, 256], F32,
                                     kind="ExternalOutput").ap()
                dbat = nc.dram_tensor("dbg_at", [8, 2 * TPC], F32,
                                      kind="ExternalOutput").ap()
                P.dma("pool", dbt, at2[:], "dbg", reads=[at2], writes=["dbg_at2"])
                P.dma("pool", dbat, a_t[:], "dbg", reads=[a_t], writes=["dbg_at"])
                dba = nc.dram_tensor("dbg_ap", [8, 2 * TPC], F32,
                                     kind="ExternalOutput").ap()
                dbm = nc.dram_tensor("dbg_msk", [8, 2 * TPC], F32,
                                     kind="ExternalOutput").ap()
                dbu = nc.dram_tensor("dbg_dstu", [128, 256], U32,
                                     kind="ExternalOutput").ap()
                P.dma("pool", dba, a_p[:], "dbg", reads=[a_p], writes=["dbg_ap"])
                P.dma("pool", dbm, msk[:], "dbg", reads=[msk], writes=["dbg_msk"])
                P.dma("pool", dbu, dstu[:], "dbg", reads=[dstu], writes=["dbg_dstu"])
            P.barrier("f1_p3")

        with ExitStack() as scC:
            entC = scC.enter_context
            exl = entC(nc.sbuf_tensor("exl", [128, 16 * EXTW], F32))
            for half in range(2):
                P.dma("sp", exl[:].rearrange("p (n c) -> p n c", c=EXTW),
                      age.rearrange("(n p) c -> p n c", p=128)
                      [:, half * 16:(half + 1) * 16, :],
                      "exr", reads=["age"], writes=[exl], tag=f"f1_exr{half}")
                for j in range(4):
                    for st in range(16):
                        gst = half * 16 + st
                        P.indirect_dma(
                            xind[j],
                            bass.IndirectOffsetOnAxis(
                                ap=dstu[:, gst * 8 + j:gst * 8 + j + 1], axis=0),
                            exl[:, st * EXTW:(st + 1) * EXTW], None,
                            f"sc_{j}", reads=[exl, dstu], writes=[f"xind{j}"],
                            tag=f"f1_sc{half}_{j}_{st}")
            P.barrier("f1_p4")

        with ExitStack() as scZ:
            entZ = scZ.enter_context
            zsb = entZ(nc.sbuf_tensor("zsb", [128, D], F32))
            for dh in range(2):
                P.op("dve", lambda v, dh=dh: v.memset(zsb[:, dh * 512:(dh + 1) * 512],
                                                      0.0),
                     writes=[zsb], tag=f"f1_z{dh}")
            for st in range(32):
                P.dma("pool", acc.rearrange("(n p) c -> p n c", p=128)[:, st, :],
                      zsb[:], "accz", reads=[zsb], writes=["acc"], tag=f"f1_az{st}")

            NIB = 8
            xit = entZ(nc.sbuf_tensor("xit", [128, 8 * EXTW], F32))
            xfm = entZ(nc.sbuf_tensor("xfm", [128, DC * 1024], F32))
            wgs = entZ(nc.sbuf_tensor("wgs", [128, DC * 512], F32))
            wus = entZ(nc.sbuf_tensor("wus", [128, DC * 512], F32))
            wds = entZ(nc.sbuf_tensor("wds", [128, 4 * 1024], F32))
            hsb = entZ(nc.sbuf_tensor("hsb", [128, 4 * 1024], F32))
            ysb = entZ(nc.sbuf_tensor("ysb", [128, DC * 1024], F32))
            gj = entZ(nc.sbuf_tensor("gj", [128, 8], F32))
            tloc = entZ(nc.sbuf_tensor("tloc", [128, 8], U32))
            g8 = entZ(nc.sbuf_tensor("g8", [128, 8], F32))
            ps_g = [entZ(nc.psum_tensor(f"ps_g{i}", [128, 512], F32)) for i in range(2)]
            ps_u2 = [entZ(nc.psum_tensor(f"ps_u2{i}", [128, 512], F32)) for i in range(2)]
            ps_y2 = [entZ(nc.psum_tensor(f"ps_y2{i}", [128, 512], F32)) for i in range(2)]
            for j in range(4):
                P.dma("sp", xit[:].rearrange("p (n c) -> p n c", c=EXTW),
                      xind[j][0:1024, :].rearrange("(n p) c -> p n c", p=128),
                      "xir", reads=[f"xind{j}"], writes=[xit], tag=f"f1_xr{j}")
                for st in range(8):
                    P.op("dve", lambda v, st=st, j=j: v.tensor_tensor(
                        g8[:], xit[:, st * EXTW + 1025:st * EXTW + 1033],
                        selm_sb[:, j * 8:(j + 1) * 8], ALU.mult),
                         reads=[xit, selm_sb], writes=[g8], tag=f"f1_g8{j}_{st}")
                    P.op("dve", lambda v, st=st: v.reduce_sum(
                        gj[:, st:st + 1], g8[:], axis=mybir.AxisListType.X),
                         reads=[g8], writes=[gj], tag=f"f1_gj{j}_{st}")
                    P.op("dve", lambda v, st=st: v.tensor_copy(
                        tloc[:, st:st + 1], xit[:, st * EXTW + 1024:st * EXTW + 1025]
                        .bitcast(U32) if False else
                        xit[:, st * EXTW + 1024:st * EXTW + 1025]),
                         reads=[xit], writes=[tloc], tag=f"f1_tl{j}_{st}")
                for st in range(8):
                    for dc in range(DC):
                        pslot = (st * 8 + dc) % 2
                        P.op("pe", lambda t, st=st, dc=dc, pslot=pslot: t.transpose(
                            ps_a[pslot][:, 0:128],
                            xit[:, st * EXTW + dc * 128:st * EXTW + dc * 128 + 128],
                            E.ident_sb[:]),
                             reads=[xit, E.ident_sb], writes=[ps_a[pslot]],
                             tag=f"f1_xt{j}_{st}_{dc}")
                        P.op("act", lambda a, st=st, dc=dc, pslot=pslot: a.copy(
                            xfm[:, dc * 1024 + st * 128:dc * 1024 + st * 128 + 128],
                            ps_a[pslot][:, 0:128]),
                             reads=[ps_a[pslot]], writes=[xfm],
                             tag=f"f1_xte{j}_{st}_{dc}")
                for ib in range(NIB):
                    P.dma("sp", wgs[:].rearrange("p (c f) -> p c f", c=DC),
                          wg_d[j].rearrange("(c p) f -> p c f", p=128)
                          [:, :, ib * 512:(ib + 1) * 512],
                          "lwg", writes=[wgs], tag=f"f1_lwg{j}_{ib}")
                    P.dma("sp", wus[:].rearrange("p (c f) -> p c f", c=DC),
                          wu_d[j].rearrange("(c p) f -> p c f", p=128)
                          [:, :, ib * 512:(ib + 1) * 512],
                          "lwu", writes=[wus], tag=f"f1_lwu{j}_{ib}")
                    P.dma("sp", wds[:].rearrange("p (c f) -> p c f", c=4),
                          wd_d[j].rearrange("(c p) f -> p c f", p=128)
                          [:, ib * 4:(ib + 1) * 4, :],
                          "lwd", writes=[wds], tag=f"f1_lwd{j}_{ib}")
                    for ic in range(4):
                        for sh in range(2):
                            pg = (ic * 2 + sh) % 2
                            for dc in range(DC):
                                P.op("pe", lambda t, ic=ic, sh=sh, dc=dc, pg=pg:
                                     t.matmul(
                                         ps_g[pg][:],
                                         wgs[:, dc * 512 + ic * 128:
                                             dc * 512 + ic * 128 + 128],
                                         xfm[:, dc * 1024 + sh * 512:
                                             dc * 1024 + sh * 512 + 512],
                                         start=(dc == 0), stop=(dc == DC - 1)),
                                     reads=[wgs, xfm], writes=[ps_g[pg]],
                                     psum_accum=(dc > 0), tag=f"f1_g{j}_{ib}_{ic}_{sh}_{dc}")
                                P.op("pe", lambda t, ic=ic, sh=sh, dc=dc, pg=pg:
                                     t.matmul(
                                         ps_u2[pg][:],
                                         wus[:, dc * 512 + ic * 128:
                                             dc * 512 + ic * 128 + 128],
                                         xfm[:, dc * 1024 + sh * 512:
                                             dc * 1024 + sh * 512 + 512],
                                         start=(dc == 0), stop=(dc == DC - 1)),
                                     reads=[wus, xfm], writes=[ps_u2[pg]],
                                     psum_accum=(dc > 0), tag=f"f1_u{j}_{ib}_{ic}_{sh}_{dc}")
                            hsl = slice(ic * 1024 + sh * 512, ic * 1024 + sh * 512 + 512)
                            P.op("act", lambda a, hsl=hsl, pg=pg: a.activation(
                                hsb[:, hsl], ps_g[pg][:], AF.Silu),
                                 reads=[ps_g[pg]], writes=[hsb],
                                 tag=f"f1_si{j}_{ib}_{ic}_{sh}")
                            P.op("dve", lambda v, hsl=hsl, pg=pg: v.tensor_tensor(
                                hsb[:, hsl], hsb[:, hsl], ps_u2[pg][:], ALU.mult),
                                 reads=[hsb, ps_u2[pg]], writes=[hsb],
                                 tag=f"f1_hm{j}_{ib}_{ic}_{sh}")
                    for dc2 in range(DC):
                        for sh in range(2):
                            py = (dc2 * 2 + sh) % 2
                            for ic in range(4):
                                P.op("pe", lambda t, dc2=dc2, sh=sh, ic=ic, py=py:
                                     t.matmul(
                                         ps_y2[py][:],
                                         wds[:, ic * 1024 + dc2 * 128:
                                             ic * 1024 + dc2 * 128 + 128],
                                         hsb[:, ic * 1024 + sh * 512:
                                             ic * 1024 + sh * 512 + 512],
                                         start=(ic == 0), stop=(ic == 3)),
                                     reads=[wds, hsb], writes=[ps_y2[py]],
                                     psum_accum=(ic > 0), tag=f"f1_y{j}_{ib}_{dc2}_{sh}_{ic}")
                            ysl = slice(dc2 * 1024 + sh * 512,
                                        dc2 * 1024 + sh * 512 + 512)
                            if ib == 0:
                                P.op("act", lambda a, ysl=ysl, py=py: a.copy(
                                    ysb[:, ysl], ps_y2[py][:]),
                                     reads=[ps_y2[py]], writes=[ysb],
                                     tag=f"f1_ye{j}_{ib}_{dc2}_{sh}")
                            else:
                                P.op("dve", lambda v, ysl=ysl, py=py: v.tensor_tensor(
                                    ysb[:, ysl], ysb[:, ysl], ps_y2[py][:], ALU.add),
                                     reads=[ysb, ps_y2[py]], writes=[ysb],
                                     tag=f"f1_ya{j}_{ib}_{dc2}_{sh}")
                for st in range(8):
                    for dc in range(DC):
                        pslot = (st * 8 + dc) % 2
                        P.op("pe", lambda t, st=st, dc=dc, pslot=pslot: t.transpose(
                            ps_a[pslot][:, 0:128],
                            ysb[:, dc * 1024 + st * 128:dc * 1024 + st * 128 + 128],
                            E.ident_sb[:]),
                             reads=[ysb, E.ident_sb], writes=[ps_a[pslot]],
                             tag=f"f1_yt{j}_{st}_{dc}")
                        P.op("act", lambda a, st=st, dc=dc, pslot=pslot: a.activation(
                            xit[:, st * EXTW + dc * 128:st * EXTW + dc * 128 + 128],
                            ps_a[pslot][:, 0:128], AF.Copy, scale=gj[:, st:st + 1]),
                             reads=[ps_a[pslot], gj], writes=[xit],
                             tag=f"f1_yte{j}_{st}_{dc}")
                    P.op("dve", lambda v, st=st: v.tensor_copy(
                        tloc[:, st:st + 1].bitcast(U32) if False else
                        tloc[:, st:st + 1],
                        tloc[:, st:st + 1]),
                         reads=[tloc], writes=[tloc], tag=f"f1_nop{j}_{st}")                         if False else None
                    P.indirect_dma(
                        acc,
                        bass.IndirectOffsetOnAxis(ap=tloc[:, st:st + 1], axis=0),
                        xit[:, st * EXTW:st * EXTW + 1024], None,
                        "sca", reads=[xit, tloc], writes=["acc"],
                        tag=f"f1_sa{j}_{st}", compute_op=ALU.add)
            P.barrier("f1_p5")

        if DBG_FFN1:
            dbx = nc.dram_tensor("dbg_xi", [128, EXTW], F32,
                                 kind="ExternalOutput").ap()
            dga = nc.dram_tensor("dbg_acc", [128, D], F32,
                                 kind="ExternalOutput").ap()
            with ExitStack() as scDD:
                tmpd = scDD.enter_context(nc.sbuf_tensor("tmpd", [128, EXTW], F32))
                P.dma("sp", tmpd[:], xind[0][0:128, :], "dbgr", reads=["xind0"],
                      writes=[tmpd], tag="dbgxi")
                P.dma("pool", dbx, tmpd[:], "dbg", reads=[tmpd], writes=["dbg_xi"])
                P.dma("sp", tmpd[:, 0:D], acc[0:128, :], "dbgr", reads=["acc"],
                      writes=[tmpd], tag="dbgacc")
                P.dma("pool", dga, tmpd[:, 0:D], "dbg", reads=[tmpd],
                      writes=["dbg_acc"])
                P.barrier("f1_dbg")

        P.collective("AllReduce", ALU.add, [acc], [accr],
                     reads=["acc"], writes=["accr"], tag="f1_ar")
        with ExitStack() as scD:
            entD = scD.enter_context
            act_sb = entD(nc.sbuf_tensor("act_sb", [128, D], F32))
            oc3 = [entD(nc.sbuf_tensor(f"oc3{i}", [128, 128], F32)) for i in range(2)]
            ps_f = [entD(nc.psum_tensor(f"ps_f{i}", [128, 512], F32)) for i in range(2)]
            xin_r = xin.rearrange("(c p) t -> p c t", p=128)
            xout_r = xout.rearrange("(c p) t -> p c t", p=128)
            for st in range(16):
                P.indirect_dma(
                    act_sb[:], None, accr,
                    bass.IndirectOffsetOnAxis(ap=utid[:, st:st + 1], axis=0),
                    "acg", reads=["accr", utid], writes=[act_sb], tag=f"f1_ga{st}")
                for dc in range(DC):
                    pslot = dc % 2
                    P.op("pe", lambda t, dc=dc, pslot=pslot: t.transpose(
                        ps_f[pslot][:, 0:128], act_sb[:, dc * 128:(dc + 1) * 128],
                        E.ident_sb[:]),
                         reads=[act_sb, E.ident_sb], writes=[ps_f[pslot]],
                         tag=f"f1_ft{st}_{dc}")
                    P.dma("sp", oc3[pslot][:], xin_r[:, dc, st * 128:(st + 1) * 128],
                          f"lx3_{pslot}", writes=[oc3[pslot]], tag=f"f1_fl{st}_{dc}")
                    P.op("dve", lambda v, pslot=pslot: v.scalar_tensor_tensor(
                        oc3[pslot][:], ps_f[pslot][:, 0:128], 0.5, oc3[pslot][:],
                        ALU.mult, ALU.add),
                         reads=[ps_f[pslot], oc3[pslot]], writes=[oc3[pslot]],
                         tag=f"f1_fc{st}_{dc}")
                    P.dma("pool", xout_r[:, dc, st * 128:(st + 1) * 128],
                          oc3[pslot][:], f"sx3_{pslot}", reads=[oc3[pslot]],
                          writes=["xout"], tag=f"f1_fs{st}_{dc}")

        P.wait_all_stores("pool")
        P.finalize(block)
    return nc



def shard_fm(x):
    out = []
    for b in range(NB):
        for h in range(2):
            out.append(np.ascontiguousarray(
                x[b, h * TPC:(h + 1) * TPC, :].T))
    return out


def unshard_fm(shards):
    x = np.empty((NB, 2 * TPC, D), np.float32)
    for b in range(NB):
        for h in range(2):
            x[b, h * TPC:(h + 1) * TPC, :] = shards[b * 2 + h].T
    return x


def run_ffn0(nc, xshards, I):
    com = common_host_inputs()
    in_maps = []
    for c in range(NCORES):
        m = dict(com)
        m["xin"] = xshards[c]
        m["w1"] = I["L0_ffn_W1"]
        m["b1"] = I["L0_ffn_b1"].reshape(1, 256)
        m["w2"] = I["L0_ffn_W2"]
        m["b2"] = I["L0_ffn_b2"].reshape(1, 8)
        m["wa"] = I["L0_ffn_Wa"]
        m["wb"] = I["L0_ffn_Wb"]
        in_maps.append(m)
    res = run_bass_kernel_spmd(nc, in_maps, core_ids=list(range(NCORES)))
    return [res.results[c]["xout"] for c in range(NCORES)]


def _consts_ffn0():
    return {}


def _map_ffn0(I, c, xin):
    m = dict(common_host_inputs())
    m.update(xin=xin, w1=I["L0_ffn_W1"], b1=I["L0_ffn_b1"].reshape(1, 256),
             w2=I["L0_ffn_W2"], b2=I["L0_ffn_b2"].reshape(1, 8),
             wa=I["L0_ffn_Wa"], wb=I["L0_ffn_Wb"])
    return m


def _map_attn0(I, c, xin, W):
    h = c % 2
    m = dict(common_host_inputs())
    m.update(xin=xin, wwt=W["wwt"], wrt=W["wrt"],
             dec=np.array([[float(I["L0_attn_decay"])]], np.float32),
             lga=np.array([[float(I["L0_attn_log_alpha"])]], np.float32),
             ffl=np.full((128, 1), float(h), np.float32),
             c_r0=W["r0"], c_r64=W["r64"], c_caus=W["caus"], c_r255=W["r255"])
    return m


def _map_attn1(I, c, xin, W):
    h = c % 2
    m = dict(common_host_inputs())
    m.update(xin=xin, tdsb=W["tdsb"], tmv=W["tmv"], tmr=W["tmr"],
             wvt=W["wvt"], wrt=W["wrt1"], wot=W["wot"],
             ffl=np.full((128, 1), float(h), np.float32), c_r0=W["r0"])
    return m


def _map_ffn1(I, c, xin, W):
    h = c % 2
    m = dict(common_host_inputs())
    m.update(xin=xin, gwt=W["gwt"], wgT=W["wgT"][h], wuT=W["wuT"][h],
             wdT=W["wdT"][h], prm=W["prm"][h], selm=W["selm"][h],
             hoff=np.full((128, 1), float(h * 2048), np.float32))
    return m


def _prep_weights(I):
    W = {}
    W["wwt"] = np.ascontiguousarray(I["L0_attn_W_write"].T)
    W["wrt"] = np.ascontiguousarray(I["L0_attn_W_read"].T)
    W["r0"] = np.broadcast_to(np.arange(TPC, dtype=np.float32), (128, TPC)).copy()
    q = np.arange(64, dtype=np.float32)
    W["r64"] = (q[None, :] - q[:, None] - 1.0).astype(np.float32)
    W["caus"] = (q[None, :] > q[:, None]).astype(np.float32)
    p = np.arange(128, dtype=np.float32)
    W["r255"] = np.stack([255.0 - p, 255.0 - (128.0 + p)], 1).astype(np.float32)
    for nm, key in (("tdsb", "L1_attn_time_decay"), ("tmv", "L1_attn_time_mix_v"),
                    ("tmr", "L1_attn_time_mix_r")):
        W[nm] = np.ascontiguousarray(I[key].reshape(8, 128).T)
    W["wvt"] = np.ascontiguousarray(I["L1_attn_Wv"].T)
    W["wrt1"] = np.ascontiguousarray(I["L1_attn_Wr"].T)
    W["wot"] = np.ascontiguousarray(I["L1_attn_Wo"].T)
    W["gwt"] = np.ascontiguousarray(I["L1_ffn_gate_w"].T)
    W["wgT"], W["wuT"], W["wdT"], W["prm"], W["selm"] = [], [], [], [], []
    for h in range(2):
        W["wgT"].append(np.ascontiguousarray(
            I["L1_ffn_w_gate"][4 * h:4 * h + 4].transpose(0, 2, 1)))
        W["wuT"].append(np.ascontiguousarray(
            I["L1_ffn_w_up"][4 * h:4 * h + 4].transpose(0, 2, 1)))
        W["wdT"].append(np.ascontiguousarray(
            I["L1_ffn_w_down"][4 * h:4 * h + 4].transpose(0, 2, 1)))
        perm = list(range(4 * h, 4 * h + 4)) +             [e for e in range(8) if not (4 * h <= e < 4 * h + 4)]
        prm = np.zeros((8, 8), np.float32)
        for i, e in enumerate(perm):
            prm[e, i] = 1.0
        W["prm"].append(prm)
        selm = np.zeros((128, 32), np.float32)
        for j in range(4):
            selm[:, j * 8 + 4 * h + j] = 1.0
        W["selm"].append(selm)
    return W


def kernel(**inputs):
    I = inputs
    W = _prep_weights(I)
    xsh = shard_fm(I["L0_attn_out"].astype(np.float32))
    cores = list(range(NCORES))

    nc_f0 = build_ffn0()
    nc_a0 = build_attn0()
    nc_f1 = build_ffn1()
    nc_a1 = build_attn1()

    def run(nc, mapfn, xcur):
        maps = [mapfn(I, c, xcur[c]) for c in cores]
        res = run_bass_kernel_spmd(nc, maps, core_ids=cores)
        return [np.ascontiguousarray(res.results[c]["xout"]) for c in cores]

    x = xsh
    x = run(nc_f0, lambda I, c, xi: _map_ffn0(I, c, xi), x)
    x = run(nc_a0, lambda I, c, xi: _map_attn0(I, c, xi, W), x)
    x = run(nc_f0, lambda I, c, xi: _map_ffn0(I, c, xi), x)
    x = run(nc_f1, lambda I, c, xi: _map_ffn1(I, c, xi, W), x)
    x = run(nc_a1, lambda I, c, xi: _map_attn1(I, c, xi, W), x)
    x = run(nc_f1, lambda I, c, xi: _map_ffn1(I, c, xi, W), x)
    return unshard_fm(x).astype(np.float32)
